# version 18
# baseline (speedup 1.0000x reference)
import contextlib
import numpy as np

import concourse.bass as bass
import concourse.mybir as mybir
from concourse.bass_utils import run_bass_kernel_spmd

F32 = mybir.dt.float32
F32R = mybir.dt.float32r
BF16 = mybir.dt.bfloat16
AF = mybir.ActivationFunctionType
OP = mybir.AluOpType
AX = mybir.AxisListType

ENGINES = ("sync", "gpsimd", "tensor", "vector", "scalar")

D = 1024
H = 16
DH = 64
NE = 8
NI = 4096
EPS = 1e-5


def batcher8_pairs():
    return [(0, 1), (2, 3), (4, 5), (6, 7),
            (0, 2), (1, 3), (4, 6), (5, 7),
            (1, 2), (5, 6),
            (0, 4), (1, 5), (2, 6), (3, 7),
            (2, 4), (3, 5),
            (1, 2), (3, 4), (5, 6)]


class Ctx:
    def __init__(self, nc, stack):
        self.nc = nc
        self.stack = stack
        self.streams = {e: [] for e in ENGINES}
        self._dirty = {}
        self.sems = {}
        self.tot = {}
        self.dma_sems = set()
        for name in ("sT", "sV", "sS", "sB"):
            self.sems[name] = stack.enter_context(nc.semaphore(name))
            self.tot[name] = 0

    def _get_sem(self, name):
        if name not in self.sems:
            self.sems[name] = self.stack.enter_context(self.nc.semaphore(name))
            self.tot[name] = 0
        return self.sems[name]

    def emit(self, engine, fn):
        self.streams[engine].append(fn)

    def op(self, engine, method, *args, inc=None, incby=1, **kwargs):
        h = self.sems[inc] if inc else None
        if engine in ("vector", "scalar"):
            if self._dirty.get(engine):
                self.emit(engine, lambda eng: eng.drain())
            self._dirty[engine] = method != "nop"

        def fn(eng, method=method, args=args, kwargs=kwargs, h=h, incby=incby):
            inst = getattr(eng, method)(*args, **kwargs)
            if h is not None:
                inst.then_inc(h, incby)

        self.emit(engine, fn)
        if inc:
            self.tot[inc] += incby
            return self.tot[inc]
        return None

    def wait(self, engine, sem, v):
        assert v is not None and v <= self.tot[sem], (engine, sem, v, self.tot[sem])
        if v <= 0:
            return
        h = self.sems[sem]
        self.emit(engine, lambda eng, h=h, v=v: eng.wait_ge(h, v))

    def waitp(self, engine, pair):
        if pair is not None:
            self.wait(engine, pair[0], pair[1])

    def dma(self, qeng, out, in_, sem):
        self._get_sem(sem)
        self.dma_sems.add(sem)
        return self.op(qeng, "dma_start", out=out, in_=in_, inc=sem, incby=16)

    def dwait(self, engine, sem):
        self._get_sem(sem)
        self.dma_sems.add(sem)
        self.wait(engine, sem, self.tot[sem])

    def barrier(self):
        for e in ENGINES:
            if e != "sync":
                self.emit(e, lambda eng: eng.drain())
                self._dirty[e] = False
            self.op(e, "nop", inc="sB")
        vb = self.tot["sB"]
        for e in ENGINES:
            self.wait(e, "sB", vb)
            for s in sorted(self.dma_sems):
                self.wait(e, s, self.tot[s])

    def finish(self):
        for e in ENGINES:
            for s in sorted(self.dma_sems):
                self.wait(e, s, self.tot[s])


class Bank:
    def __init__(self, ap):
        self.ap = ap
        self.ready = None
        self.freed = None


def mm3(ctx, ps_ap, shi, slo, mhi, mlo, start, stop, inc=None):
    ctx.op("tensor", "matmul", ps_ap, shi, mhi, start=start, stop=False)
    ctx.op("tensor", "matmul", ps_ap, shi, mlo, start=False, stop=False)
    return ctx.op("tensor", "matmul", ps_ap, slo, mhi, start=False, stop=stop,
                  inc=inc)


def build_program(S=8192, TOK=2048, debug=(), stages=99):
    nc = bass.Bass("TRN2", target_bir_lowering=False)
    NTF = S // 128
    NT = TOK // 128
    NQT = TOK // 512
    NKC = S // 128
    NKCH = NKC // 2
    NSB = TOK // 512
    assert TOK % 512 == 0 and S % 256 == 0 and NT <= NTF

    def dram_in(name, shape, dtype=F32):
        return nc.dram_tensor(name, shape, dtype, kind="ExternalInput").ap()

    def scratch(name, shape, dtype=F32):
        kind = "ExternalOutput" if name in debug else "Internal"
        return nc.dram_tensor(name, shape, dtype, kind=kind).ap()

    xf = dram_in("xf", [S, D])
    xo = dram_in("xo", [TOK, D])
    wqT = dram_in("wqT", [D, D])
    wkT = dram_in("wkT", [D, D])
    wvT = dram_in("wvT", [D, D])
    wffT = dram_in("wffT", [D, D])
    bffi = dram_in("bff", [1, D])
    wrT = dram_in("wrT", [D, NE])
    w1T = dram_in("w1T", [NE, D, NI])
    w2T = dram_in("w2T", [NE, NI, D])
    ident_in = dram_in("ident", [128, 128])

    y = nc.dram_tensor("y", [TOK, D], F32, kind="ExternalOutput").ap()

    kt_scr = scratch("kt_scr", [D, S])
    v_scr = scratch("v_scr", [S, D])
    qt_scr = scratch("qt_scr", [D, TOK])
    o2_scr = scratch("o2_scr", [NT, D, 128])
    xm_scr = scratch("xm_scr", [TOK, D])
    w1hi_scr = scratch("w1hi_scr", [NE, D, NI], BF16)
    w1lo_scr = scratch("w1lo_scr", [NE, D, NI], BF16)
    w2hi_scr = scratch("w2hi_scr", [NE, NI, D], BF16)
    w2lo_scr = scratch("w2lo_scr", [NE, NI, D], BF16)

    with contextlib.ExitStack() as stack:
        E = stack.enter_context
        ctx = Ctx(nc, stack)

        ident = E(nc.sbuf_tensor("identsb", [128, 128], F32))
        ones = E(nc.sbuf_tensor("ones", [1, 128], F32))
        bffb = E(nc.sbuf_tensor("bffb", [128, D], F32))
        a1 = E(nc.sbuf_tensor("a1", [128, NTF], F32))
        b1 = E(nc.sbuf_tensor("b1", [128, NTF], F32))
        a2 = E(nc.sbuf_tensor("a2", [128, NT], F32))
        b2 = E(nc.sbuf_tensor("b2", [128, NT], F32))
        wrh = E(nc.sbuf_tensor("wrh", [128, 8 * NE], BF16))
        wrl = E(nc.sbuf_tensor("wrl", [128, 8 * NE], BF16))
        xb = [E(nc.sbuf_tensor(f"xb{i}", [128, D], F32)) for i in range(2)]
        stat = E(nc.sbuf_tensor("stat", [128, 16], F32))
        mv = E(nc.sbuf_tensor("mv", [128, 8], F32))

        pb = [Bank(E(nc.psum_tensor(f"pb{i}", [128, 512], F32)))
              for i in range(8)]

        wsplit_jobs = []
        for e in range(NE):
            for r in range(D // 128):
                for cc in range(NI // 1024):
                    rs = slice(128 * r, 128 * (r + 1))
                    cs = slice(1024 * cc, 1024 * (cc + 1))
                    wsplit_jobs.append((w1T[e, rs, cs], w1hi_scr[e, rs, cs],
                                        w1lo_scr[e, rs, cs]))
            for r in range(NI // 128):
                rs = slice(128 * r, 128 * (r + 1))
                wsplit_jobs.append((w2T[e, rs, :], w2hi_scr[e, rs, :],
                                    w2lo_scr[e, rs, :]))
        ws_state = {"pi": 0, "li": 0, "busy": [False, False]}

        def wsplit_load(i):
            slot = i % 2
            srcap = wsplit_jobs[i][0]
            if ws_state["busy"][slot]:
                ctx.dwait("scalar", f"sWS{slot}")
            ctx.dma("scalar", wps[slot][:], srcap, f"sWP{slot}")
            ws_state["li"] = i + 1

        def wsplit_step():
            while (ws_state["li"] < len(wsplit_jobs)
                   and ws_state["li"] < ws_state["pi"] + 2):
                wsplit_load(ws_state["li"])
            i = ws_state["pi"]
            if i >= len(wsplit_jobs):
                return
            ws_state["pi"] = i + 1
            slot = i % 2
            srcap, dhi, dlo = wsplit_jobs[i]
            ctx.dwait("vector", f"sWP{slot}")
            ctx.op("vector", "tensor_copy", wph[slot][:], wps[slot][:])
            vl = ctx.op("vector", "tensor_tensor", out=wpl[slot][:],
                        in0=wps[slot][:], in1=wph[slot][:], op=OP.subtract,
                        inc="sV")
            ctx.wait("gpsimd", "sV", vl)
            ctx.dma("gpsimd", dhi, wph[slot][:], f"sWS{slot}")
            ctx.dma("gpsimd", dlo, wpl[slot][:], f"sWS{slot}")
            ws_state["busy"][slot] = True

        def ln_stats(xb_ap, a_col, b_col):
            ctx.op("vector", "bn_stats", stat[:, 0:6], xb_ap[:, 0:512])
            ctx.op("vector", "bn_stats", stat[:, 6:12], xb_ap[:, 512:1024])
            ctx.op("vector", "bn_aggr", mv[:, 0:2], stat[:, 0:12])
            ctx.op("vector", "tensor_scalar", out=mv[:, 2:3], in0=mv[:, 1:2],
                   scalar1=EPS, scalar2=None, op0=OP.add)
            vv = ctx.op("vector", "reciprocal", mv[:, 3:4], mv[:, 2:3],
                        inc="sV")
            ctx.wait("scalar", "sV", vv)
            vs = ctx.op("scalar", "activation", a_col, mv[:, 3:4], AF.Sqrt,
                        inc="sS")
            ctx.wait("vector", "sS", vs)
            return ctx.op("vector", "scalar_tensor_tensor", out=b_col,
                          in0=mv[:, 0:1], scalar=-1.0, in1=a_col,
                          op0=OP.mult, op1=OP.mult, inc="sV")

        def transpose8(src_sb, banks, dst_f32, wait_tensor=()):
            for sem, v in wait_tensor:
                ctx.wait("tensor", sem, v)
            for g in range(2):
                bk = banks[g]
                ctx.waitp("tensor", bk.freed)
                for q in range(4):
                    c = g * 4 + q
                    vt = ctx.op("tensor", "transpose",
                                bk.ap[:, 128 * q:128 * (q + 1)],
                                src_sb[:, 128 * c:128 * (c + 1)], ident[:],
                                inc="sT" if q == 3 else None)
                bk.ready = ("sT", vt)
            vv = None
            for g in range(2):
                bk = banks[g]
                ctx.waitp("vector", bk.ready)
                vv = ctx.op("vector", "tensor_copy",
                            dst_f32[:, 512 * g:512 * (g + 1)], bk.ap[:],
                            inc="sV")
                bk.freed = ("sV", vv)
            return vt, vv

        def split_f32r(dst_hi, dst_lo, src, n):
            ctx.op("vector", "tensor_copy", dst_hi[:, 0:n], src[:, 0:n])
            return ctx.op("vector", "tensor_tensor", out=dst_lo[:, 0:n],
                          in0=src[:, 0:n], in1=dst_hi[:, 0:n],
                          op=OP.subtract, inc="sV")

        ctx.dma("sync", ident[:], ident_in, "sIN")
        ctx.dma("sync", xb[0][0:1, :], bffi, "sIN")
        ctx.dma("sync",
                xb[1][:, 0:8 * NE].rearrange("p (c e) -> p c e", e=NE),
                wrT.rearrange("(c p) e -> p c e", p=128), "sIN")
        ctx.op("vector", "memset", ones[:], 1.0)
        ctx.dwait("vector", "sIN")
        ctx.op("vector", "tensor_copy", wrh[:, 0:8 * NE], xb[1][:, 0:8 * NE])
        ctx.op("vector", "tensor_tensor", out=wrl[:, 0:8 * NE],
               in0=xb[1][:, 0:8 * NE], in1=wrh[:, 0:8 * NE], op=OP.subtract)
        vv0 = ctx.op("vector", "nop", inc="sV")
        ctx.dwait("tensor", "sIN")
        ctx.wait("tensor", "sV", vv0)
        vt0 = None
        for g in range(2):
            vt0 = ctx.op("tensor", "matmul", pb[6 + g].ap[:, :], ones[0:1, :],
                         xb[0][0:1, 512 * g:512 * (g + 1)], start=True,
                         stop=True, inc="sT")
        ctx.wait("vector", "sT", vt0)
        for g in range(2):
            ctx.op("vector", "tensor_copy", bffb[:, 512 * g:512 * (g + 1)],
                   pb[6 + g].ap[:])
        ctx.op("vector", "nop", inc="sV")
        ctx.barrier()

        if stages < 1:
            ctx.finish()
            return finalize(nc, ctx)

        xln = [E(nc.sbuf_tensor(f"xln{i}", [128, D], F32)) for i in range(2)]
        wsps = contextlib.ExitStack()
        EW = wsps.enter_context
        wps = [EW(nc.sbuf_tensor(f"wps{i}", [128, 1024], F32)) for i in range(2)]
        wph = [EW(nc.sbuf_tensor(f"wph{i}", [128, 1024], BF16)) for i in range(2)]
        wpl = [EW(nc.sbuf_tensor(f"wpl{i}", [128, 1024], BF16)) for i in range(2)]
        p2s = contextlib.ExitStack()
        E2 = p2s.enter_context
        wWh = E2(nc.sbuf_tensor("wWh", [128, 8 * D], BF16))
        wWl = E2(nc.sbuf_tensor("wWl", [128, 8 * D], BF16))
        wraw = E2(nc.sbuf_tensor("wraw", [128, 8 * D], F32))
        xlT = [E2(nc.sbuf_tensor(f"xlT{i}", [128, D], F32)) for i in range(2)]
        xlh = [E2(nc.sbuf_tensor(f"xlh{i}", [128, D], BF16)) for i in range(2)]
        xll = [E2(nc.sbuf_tensor(f"xll{i}", [128, D], BF16)) for i in range(2)]
        ko = [E2(nc.sbuf_tensor(f"ko{i}", [128, D], F32)) for i in range(2)]
        vo = [E2(nc.sbuf_tensor(f"vo{i}", [128, D], F32)) for i in range(2)]
        koT = [E2(nc.sbuf_tensor(f"koT{i}", [128, D], F32)) for i in range(2)]
        wWh2 = E2(nc.sbuf_tensor("wWh2", [128, 8 * D], BF16))
        wWl2 = E2(nc.sbuf_tensor("wWl2", [128, 8 * D], BF16))

        def load_weight_T(wT, dh=None, dl=None):
            ctx.wait("sync", "sV", ctx.tot["sV"])
            ctx.dma("sync",
                    wraw[:].rearrange("p (c o) -> p c o", o=D),
                    wT.rearrange("(c p) o -> p c o", p=128), "sIN")
            ctx.dwait("vector", "sIN")
            return split_f32r(dh if dh is not None else wWh,
                              dl if dl is not None else wWl, wraw, 8 * D)

        def proj_phase(kind):
            fused = kind == "KV"
            wT = {"KV": wkT, "Q": wqT}[kind]
            ntiles = NTF if fused else NT
            xsrc = xf if fused else xo
            vw = load_weight_T(wT)
            if fused:
                vw = load_weight_T(wvT, wWh2, wWl2)
            ctx.wait("tensor", "sV", vw)
            last_ev = {}
            store_done = {}
            tdone = {}

            def stage_a(t):
                sl = t % 2
                if (t - 2) in last_ev:
                    ctx.wait("sync", "sS", last_ev[t - 2])
                ctx.dma("sync", xb[sl][:], xsrc[128 * t:128 * (t + 1), :],
                        f"sX{sl}")
                ctx.dwait("vector", f"sX{sl}")
                vb = ln_stats(xb[sl][:], a1[:, t:t + 1], b1[:, t:t + 1])
                ctx.wait("scalar", "sV", vb)
                ctx.dwait("scalar", f"sX{sl}")
                if (t - 2) in tdone:
                    ctx.wait("scalar", "sT", tdone[t - 2])
                last_ev[t] = ctx.op("scalar", "activation", xln[sl][:],
                                    xb[sl][:], AF.Identity,
                                    bias=b1[:, t:t + 1], scale=a1[:, t:t + 1],
                                    inc="sS")

            def stage_kt(t):
                sl = t % 2
                if (t - 2) in store_done:
                    ctx.dwait("vector", f"sST{sl}")
                _, vvk = transpose8(ko[sl][:], [pb[4], pb[5]], koT[sl][:],
                                    wait_tensor=[("sS", evdone[t])])
                ctx.wait("gpsimd", "sV", vvk)
                dst = kt_scr if kind == "KV" else qt_scr
                for c in range(8):
                    ctx.dma("gpsimd",
                            dst[128 * c:128 * (c + 1), 128 * t:128 * (t + 1)],
                            koT[sl][:, 128 * c:128 * (c + 1)], f"sST{sl}")
                store_done[t] = True

            evdone = {}
            spl = {}
            stage_a(0)
            tdone[0], _ = transpose8(xln[0][:], [pb[6], pb[7]], xlT[0][:],
                                     wait_tensor=[("sS", last_ev[0])])
            spl[0] = split_f32r(xlh[0], xll[0], xlT[0][:], D)
            for t in range(ntiles):
                sl = t % 2
                if t + 1 < ntiles:
                    nsl = (t + 1) % 2
                    stage_a(t + 1)
                    tdone[t + 1], _ = transpose8(
                        xln[nsl][:], [pb[6], pb[7]], xlT[nsl][:],
                        wait_tensor=[("sS", last_ev[t + 1])])
                    spl[t + 1] = split_f32r(xlh[nsl], xll[nsl], xlT[nsl][:],
                                            D)
                ctx.wait("tensor", "sV", spl[t])
                groups = ([(wWh, wWl, 0, ko), (wWh, wWl, 1, ko),
                           (wWh2, wWl2, 0, vo), (wWh2, wWl2, 1, vo)]
                          if fused else [(wWh, wWl, 0, ko), (wWh, wWl, 1, ko)])
                for gi, (wh, wl, blk, dsts) in enumerate(groups):
                    bk = pb[gi]
                    ctx.waitp("tensor", bk.freed)
                    for din in range(8):
                        vt = mm3(
                            ctx, bk.ap[:, :],
                            xlh[sl][:, 128 * din:128 * (din + 1)],
                            xll[sl][:, 128 * din:128 * (din + 1)],
                            wh[:, D * din + 512 * blk:D * din + 512 * blk + 512],
                            wl[:, D * din + 512 * blk:D * din + 512 * blk + 512],
                            start=(din == 0), stop=(din == 7),
                            inc="sT" if din == 7 else None)
                    bk.ready = ("sT", vt)
                if t >= 1:
                    stage_kt(t - 1)
                if (t - 2) in store_done:
                    ctx.dwait("scalar", f"sST{sl}")
                for gi, (wh, wl, blk, dsts) in enumerate(groups):
                    bk = pb[gi]
                    ctx.waitp("scalar", bk.ready)
                    sc = (1.0 / 32.0) if kind == "Q" else 1.0
                    vs = ctx.op("scalar", "activation",
                                dsts[sl][:, 512 * blk:512 * (blk + 1)],
                                bk.ap[:], AF.Copy, scale=sc, inc="sS")
                    bk.freed = ("sS", vs)
                evdone[t] = ctx.tot["sS"]
                if fused:
                    ctx.wait("gpsimd", "sS", evdone[t])
                    ctx.dma("gpsimd", v_scr[128 * t:128 * (t + 1), :],
                            vo[sl][:], f"sST{sl}")
                store_done[t] = True
                wsplit_step()
                wsplit_step()
                if fused:
                    wsplit_step()
                    wsplit_step()
            stage_kt(ntiles - 1)

        proj_phase("KV")
        ctx.barrier()
        proj_phase("Q")
        ctx.barrier()
        p2s.close()

        if stages < 2:
            ctx.finish()
            return finalize(nc, ctx)

        SH = S // 2
        p3s = contextlib.ExitStack()
        E3 = p3s.enter_context
        kraw = [E3(nc.sbuf_tensor(f"kraw{i}", [128, SH], F32))
                for i in range(2)]
        kstk = [E3(nc.sbuf_tensor(f"kstk{i}", [128, SH], BF16))
                for i in range(2)]
        vraw = [E3(nc.sbuf_tensor(f"vraw{i}", [128, NKCH * 64], F32))
                for i in range(2)]
        vst1 = [E3(nc.sbuf_tensor(f"vst1{i}", [128, NKCH * 65], BF16))
                for i in range(2)]
        vst2 = [E3(nc.sbuf_tensor(f"vst2{i}", [128, NKCH * 64], BF16))
                for i in range(2)]
        qst = [E3(nc.sbuf_tensor(f"qst{i}", [128, TOK], F32)) for i in range(2)]
        qhi = [E3(nc.sbuf_tensor(f"qhi{i}", [128, TOK], BF16))
               for i in range(2)]
        qlo = [E3(nc.sbuf_tensor(f"qlo{i}", [128, TOK], BF16))
               for i in range(2)]
        Eb = [E3(nc.sbuf_tensor(f"Eb{i}", [128, 2048], F32))
              for i in range(2)]
        EhiB = [E3(nc.sbuf_tensor(f"Ehi{i}", [128, 2048], BF16))
                for i in range(3)]
        EloB = [E3(nc.sbuf_tensor(f"Elo{i}", [128, 2048], BF16))
                for i in range(3)]
        rrow = E3(nc.sbuf_tensor("rrow", [1, 512], F32))
        bcb = E3(nc.sbuf_tensor("bcb", [64, 512], F32))
        o2st = [E3(nc.sbuf_tensor(f"o2st{i}", [128, 512], F32))
                for i in range(2)]

        for i in range(2):
            ctx.op("vector", "memset",
                   vst1[i][:].rearrange("p (k c) -> p k c", c=65)
                   [:, :, 64:65], 1.0)
        vv_ones = ctx.op("vector", "nop", inc="sV")
        ctx.wait("gpsimd", "sV", vv_ones)
        ctx.wait("tensor", "sV", vv_ones)
        ctx._get_sem("sP")

        def gop(method, *a, **k):
            ctx.wait("gpsimd", "sP", ctx.tot["sP"])
            return ctx.op("gpsimd", method, *a, inc="sP", **k)

        kv_freed = [None, None]
        q_freed = [None, None]
        kv_ready = [None, None]
        q_ready = [None, None]

        def load_head_kv(h, half, slot):
            if kv_freed[slot] is not None:
                ctx.wait("sync", *kv_freed[slot])
            src = kt_scr[64 * h:64 * (h + 1), SH * half:SH * (half + 1)]
            ctx.dma("sync", kraw[slot][0:64, :], src, f"sKV{slot}")
            ctx.dma("sync", kraw[slot][64:128, :], src, f"sKV{slot}")
            G = 4
            for g in range(NKCH // G):
                rows = v_scr[SH * half + 128 * G * g:
                             SH * half + 128 * G * (g + 1),
                             64 * h:64 * (h + 1)]
                ctx.dma("sync",
                        vraw[slot][:].rearrange("p (k c) -> p k c", c=64)
                        [:, G * g:G * (g + 1), :],
                        rows.rearrange("(k p) c -> p k c", p=128),
                        f"sKV{slot}")
            ctx.dwait("gpsimd", f"sKV{slot}")
            gop("tensor_copy", kstk[slot][0:64, :], kraw[slot][0:64, :])
            gop("tensor_copy", kstk[slot][64:128, :], kraw[slot][64:128, :])
            gop("tensor_tensor", out=kstk[slot][64:128, :],
                in0=kraw[slot][64:128, :], in1=kstk[slot][64:128, :],
                op=OP.subtract)
            vv3 = vraw[slot][:].rearrange("p (k c) -> p k c", c=64)
            vh3 = vst1[slot][:].rearrange("p (k c) -> p k c", c=65)[:, :, 0:64]
            vl3 = vst2[slot][:].rearrange("p (k c) -> p k c", c=64)
            gop("tensor_copy", vh3, vv3)
            vv = gop("tensor_tensor", out=vl3, in0=vv3, in1=vh3,
                     op=OP.subtract)
            kv_ready[slot] = ("sP", vv)

        def load_head_q(h, slot):
            if q_freed[slot] is not None:
                ctx.wait("sync", *q_freed[slot])
            src = qt_scr[64 * h:64 * (h + 1), :]
            ctx.dma("sync", qst[slot][0:64, :], src, f"sQL{slot}")
            ctx.dma("sync", qst[slot][64:128, :], src, f"sQL{slot}")
            ctx.dwait("vector", f"sQL{slot}")
            ctx.op("vector", "tensor_copy", qhi[slot][:], qst[slot][:])
            vv = ctx.op("vector", "tensor_tensor", out=qlo[slot][:],
                        in0=qst[slot][:], in1=qhi[slot][:], op=OP.subtract,
                        inc="sV")
            q_ready[slot] = ("sV", vv)

        load_head_q(0, 0)
        load_head_kv(0, 0, 0)
        e_free2 = [None, None, None]
        eb_users = [[], []]
        ehilo_v = [None, None, None]

        for h in range(H):
            qslot = h % 2
            if h + 1 < H:
                load_head_q(h + 1, (h + 1) % 2)
            ctx.waitp("tensor", q_ready[qslot])
            for qt in range(NQT):
                ctx.waitp("tensor", pb[qt].freed)
            last_mm = None
            pipe = []
            for kci in range(NKC + 2):
                if kci < NKC:
                    half = kci // NKCH
                    kc = kci % NKCH
                    slot = (2 * h + half) % 2
                    par = kci % 3
                    if kc == 0:
                        ctx.waitp("tensor", kv_ready[slot])
                    if kc == 2:
                        if half == 0:
                            load_head_kv(h, 1, slot ^ 1)
                        elif h + 1 < H:
                            load_head_kv(h + 1, 0, slot ^ 1)
                        for _ in range(8):
                            wsplit_step()
                    kap = kstk[slot][:, 128 * kc:128 * (kc + 1)]
                    epar = kci % 2
                    for qt in range(NQT):
                        sbk = pb[4 + qt]
                        ctx.waitp("tensor", sbk.freed)
                        ctx.op("tensor", "matmul", sbk.ap[:, :], kap,
                               qhi[qslot][:, 512 * qt:512 * (qt + 1)],
                               start=True, stop=False)
                        vt = ctx.op("tensor", "matmul", sbk.ap[:, :], kap,
                                    qlo[qslot][:, 512 * qt:512 * (qt + 1)],
                                    start=False, stop=True, inc="sT")
                        sbk.ready = ("sT", vt)
                    for pr in eb_users[epar]:
                        ctx.waitp("scalar", pr)
                    for qt in range(NQT):
                        sbk = pb[4 + qt]
                        ctx.waitp("scalar", sbk.ready)
                        vexp = ctx.op("scalar", "activation",
                                      Eb[epar][:, 512 * qt:512 * (qt + 1)],
                                      sbk.ap[:], AF.Exp, inc="sS")
                        sbk.freed = ("sS", vexp)
                    if e_free2[par] is not None:
                        ctx.wait("scalar", "sT", e_free2[par])
                    vhi = ctx.op("scalar", "activation",
                                 EhiB[par][:, 0:512 * NQT],
                                 Eb[epar][:, 0:512 * NQT], AF.Copy, inc="sS")
                    ctx.wait("vector", "sS", vhi)
                    if e_free2[par] is not None:
                        ctx.wait("vector", "sT", e_free2[par])
                    vlo = ctx.op("vector", "scalar_tensor_tensor",
                                 out=EloB[par][:, 0:512 * NQT],
                                 in0=Eb[epar][:, 0:512 * NQT],
                                 scalar=1.0, in1=EhiB[par][:, 0:512 * NQT],
                                 op0=OP.mult, op1=OP.subtract, inc="sV")
                    ehilo_v[par] = (vhi, vlo)
                    eb_users[epar] = [("sS", vhi), ("sV", vlo)]
                    pipe.append((vst1[slot][:, 65 * kc:65 * kc + 65],
                                 vst2[slot][:, 64 * kc:64 * kc + 64],
                                 par, kci == 0, kci == NKC - 1,
                                 slot if kc == NKCH - 1 else None))
                if kci >= 2:
                    pvap1, pvap2, ppar, pfirst, plast, freeslot = pipe.pop(0)
                    ctx.wait("tensor", "sS", ehilo_v[ppar][0])
                    ctx.wait("tensor", "sV", ehilo_v[ppar][1])
                    for qt in range(NQT):
                        qs = slice(512 * qt, 512 * (qt + 1))
                        ctx.op("tensor", "matmul", pb[qt].ap[0:65, :], pvap1,
                               EhiB[ppar][:, qs], start=pfirst, stop=False)
                        ctx.op("tensor", "matmul", pb[qt].ap[0:64, :], pvap2,
                               EhiB[ppar][:, qs], start=False, stop=False)
                        vt2 = ctx.op("tensor", "matmul", pb[qt].ap[0:65, :],
                                     pvap1, EloB[ppar][:, qs],
                                     start=False, stop=plast, inc="sT")
                        if plast:
                            pb[qt].ready = ("sT", vt2)
                        last_mm = vt2
                    e_free2[ppar] = last_mm
                    if freeslot is not None:
                        kv_freed[freeslot] = ("sT", last_mm)
            q_freed[qslot] = ("sT", last_mm)
            for qt in range(NQT):
                bk = pb[qt]
                par = qt % 2
                ctx.waitp("vector", bk.ready)
                vvr = ctx.op("vector", "reciprocal", rrow[:],
                             bk.ap[64:65, :], inc="sV")
                sbc = pb[6 + par] if NQT == 4 else pb[5]
                ctx.waitp("tensor", sbc.freed)
                ctx.wait("tensor", "sV", vvr)
                vt2 = ctx.op("tensor", "matmul", sbc.ap[0:64, :],
                             ones[0:1, 0:64], rrow[:], start=True, stop=True,
                             inc="sT")
                sbc.ready = ("sT", vt2)
                ctx.wait("vector", "sT", vt2)
                ctx.op("vector", "tensor_copy", bcb[:], sbc.ap[0:64, :])
                ctx.dwait("vector", f"sO2{par}")
                vdv = ctx.op("vector", "tensor_tensor",
                             out=o2st[par][0:64, :],
                             in0=bk.ap[0:64, :], in1=bcb[:], op=OP.mult,
                             inc="sV")
                bk.freed = ("sV", vdv)
                sbc.freed = ("sV", vdv)
                ctx.wait("gpsimd", "sV", vdv)
                for j in range(4):
                    tt = 4 * qt + j
                    ctx.dma("gpsimd", o2_scr[tt, 64 * h:64 * (h + 1), :],
                            o2st[par][0:64, 128 * j:128 * (j + 1)],
                            f"sO2{par}")
        while ws_state["pi"] < len(wsplit_jobs):
            wsplit_step()
        ctx.barrier()
        p3s.close()
        wsps.close()

        if stages < 3:
            ctx.finish()
            return finalize(nc, ctx)

        p4s = contextlib.ExitStack()
        E4 = p4s.enter_context
        wWh = E4(nc.sbuf_tensor("wWh4", [128, 8 * D], BF16))
        wWl = E4(nc.sbuf_tensor("wWl4", [128, 8 * D], BF16))
        wraw = E4(nc.sbuf_tensor("wraw4", [128, 8 * D], F32))
        o2r = [E4(nc.sbuf_tensor(f"o2r{i}", [128, D], F32)) for i in range(2)]
        o2h = E4(nc.sbuf_tensor("o2h", [128, D], BF16))
        o2l = E4(nc.sbuf_tensor("o2l", [128, D], BF16))
        xmid = [E4(nc.sbuf_tensor(f"xmid{i}", [128, D], F32)) for i in range(2)]
        tacc = E4(nc.sbuf_tensor("tacc", [128, D], F32))

        def load_weight_ff(wT):
            ctx.dma("sync",
                    wraw[:].rearrange("p (c o) -> p c o", o=D),
                    wT.rearrange("(c p) o -> p c o", p=128), "sIN")
            ctx.dwait("vector", "sIN")
            return split_f32r(wWh, wWl, wraw, 8 * D)

        vw = load_weight_ff(wffT)
        ctx.wait("tensor", "sV", vw)
        ctx.dma("sync", o2r[0][:].rearrange("p (c t) -> p c t", t=128),
                o2_scr[0].rearrange("(c p) t -> p c t", p=128), "sOR0")
        ctx.dma("sync", xb[0][:], xo[0:128, :], "sX0")
        prev_mm = None
        split_done = {}
        xm_used = {}
        xm_stored = {}
        for t in range(NT):
            sl = t % 2
            if t + 1 < NT:
                if (t - 1) in split_done:
                    ctx.wait("sync", "sV", split_done[t - 1])
                if (t - 1) in xm_used:
                    ctx.wait("sync", "sV", xm_used[t - 1])
                ctx.dma(
                    "sync",
                    o2r[(t + 1) % 2][:].rearrange("p (c t) -> p c t", t=128),
                    o2_scr[t + 1].rearrange("(c p) t -> p c t", p=128),
                    f"sOR{(t + 1) % 2}")
                ctx.dma("sync", xb[(t + 1) % 2][:],
                        xo[128 * (t + 1):128 * (t + 2), :],
                        f"sX{(t + 1) % 2}")
            ctx.dwait("vector", f"sOR{sl}")
            if prev_mm is not None:
                ctx.wait("vector", "sT", prev_mm)
            vsp = split_f32r(o2h, o2l, o2r[sl][:], D)
            split_done[t] = vsp
            ctx.wait("tensor", "sV", vsp)
            for blk in range(2):
                bk = pb[blk]
                ctx.waitp("tensor", bk.freed)
                for hc in range(8):
                    vt = mm3(
                        ctx, bk.ap[:, :],
                        o2h[:, 128 * hc:128 * (hc + 1)],
                        o2l[:, 128 * hc:128 * (hc + 1)],
                        wWh[:, D * hc + 512 * blk:D * hc + 512 * blk + 512],
                        wWl[:, D * hc + 512 * blk:D * hc + 512 * blk + 512],
                        start=(hc == 0), stop=(hc == 7),
                        inc="sT" if hc == 7 else None)
                bk.ready = ("sT", vt)
            prev_mm = ctx.tot["sT"]
            ctx.dwait("vector", f"sX{sl}")
            if (t - 2) in xm_stored:
                ctx.dwait("vector", f"sXM{sl}")
            for blk in range(2):
                bk = pb[blk]
                ctx.waitp("vector", bk.ready)
                cs = slice(512 * blk, 512 * (blk + 1))
                ctx.op("vector", "tensor_tensor", out=tacc[:, cs],
                       in0=bk.ap[:, :], in1=bffb[:, cs], op=OP.add)
                vv = ctx.op("vector", "tensor_tensor", out=xmid[sl][:, cs],
                            in0=tacc[:, cs], in1=xb[sl][:, cs], op=OP.add,
                            inc="sV")
                bk.freed = ("sV", vv)
            xm_used[t] = ctx.tot["sV"]
            vb2 = ln_stats(xmid[sl][:], a2[:, t:t + 1], b2[:, t:t + 1])
            ctx.wait("gpsimd", "sV", vb2)
            ctx.dma("gpsimd", xm_scr[128 * t:128 * (t + 1), :],
                    xmid[sl][:], f"sXM{sl}")
            xm_stored[t] = True
        ctx.barrier()
        p4s.close()

        if stages < 4:
            ctx.finish()
            return finalize(nc, ctx)

        p5s = contextlib.ExitStack()
        E5 = p5s.enter_context
        xlnT2h = E5(nc.sbuf_tensor("xlnT2h", [128, 4 * D], BF16))
        xlnT2l = E5(nc.sbuf_tensor("xlnT2l", [128, 4 * D], BF16))
        gTh = E5(nc.sbuf_tensor("gTh", [128, 4 * NI], BF16))
        gTl = E5(nc.sbuf_tensor("gTl", [128, 4 * NI], BF16))
        w1b = [E5(nc.sbuf_tensor(f"w1b{i}", [128, 2 * NI], BF16))
               for i in range(2)]
        w2b = [E5(nc.sbuf_tensor(f"w2b{i}", [128, 8 * D], BF16))
               for i in range(2)]
        ffa = E5(nc.sbuf_tensor("ffa", [128, 4 * D], F32))
        probs = E5(nc.sbuf_tensor("probs", [128, 4 * 9], F32))
        yb = [E5(nc.sbuf_tensor(f"yb{i}", [128, D], F32)) for i in range(2)]

        for sb in range(NSB):
            ctx.barrier()
            xln_used = {}
            norm_done = {}
            for j in range(4):
                t = 4 * sb + j
                sl = j % 2
                if (j - 2) in norm_done:
                    ctx.wait("sync", "sS", norm_done[j - 2])
                ctx.dma("sync", xb[sl][:],
                        xm_scr[128 * t:128 * (t + 1), :], f"sX{sl}")
                ctx.dwait("scalar", f"sX{sl}")
                if (j - 2) in xln_used:
                    ctx.wait("scalar", "sT", xln_used[j - 2])
                vs = ctx.op("scalar", "activation", xln[sl][:], xb[sl][:],
                            AF.Identity, bias=b2[:, t:t + 1],
                            scale=a2[:, t:t + 1], inc="sS")
                norm_done[j] = vs
                ctx.wait("tensor", "sS", vs)
                for g in range(2):
                    bk = pb[6 + g]
                    ctx.waitp("tensor", bk.freed)
                    for q in range(4):
                        c = 4 * g + q
                        vt = ctx.op("tensor", "transpose",
                                    bk.ap[:, 128 * q:128 * (q + 1)],
                                    xln[sl][:, 128 * c:128 * (c + 1)],
                                    ident[:], inc="sT" if q == 3 else None)
                    bk.ready = ("sT", vt)
                xln_used[j] = ctx.tot["sT"]
                for g in range(2):
                    bk = pb[6 + g]
                    ctx.waitp("vector", bk.ready)
                    dst = slice(D * j + 512 * g, D * j + 512 * (g + 1))
                    ctx.op("vector", "tensor_copy", xlnT2h[:, dst],
                           bk.ap[:, :])
                    vfree = ctx.op("vector", "tensor_tensor",
                                   out=xlnT2l[:, dst], in0=bk.ap[:, :],
                                   in1=xlnT2h[:, dst], op=OP.subtract,
                                   inc="sV")
                    bk.freed = ("sV", vfree)
            vxt = ctx.tot["sV"]

            ctx.wait("tensor", "sV", vxt)
            for j in range(4):
                bk = pb[j]
                ctx.waitp("tensor", bk.freed)
                for din in range(8):
                    xs = slice(D * j + 128 * din, D * j + 128 * (din + 1))
                    vt = mm3(ctx, bk.ap[:, 0:NE], xlnT2h[:, xs],
                             xlnT2l[:, xs],
                             wrh[:, NE * din:NE * (din + 1)],
                             wrl[:, NE * din:NE * (din + 1)],
                             start=(din == 0), stop=(din == 7),
                             inc="sT" if din == 7 else None)
                bk.ready = ("sT", vt)
            perms = []
            for j in range(4):
                bk = pb[j]
                ctx.waitp("vector", bk.ready)
                ctx.op("vector", "reduce_max", mv[:, 0:1], bk.ap[:, 0:NE],
                       axis=AX.X)
                vv = ctx.op("vector", "tensor_scalar", out=mv[:, 1:2],
                            in0=mv[:, 0:1], scalar1=-1.0, scalar2=None,
                            op0=OP.mult, inc="sV")
                ctx.wait("scalar", "sV", vv)
                pc = slice(9 * j, 9 * j + 8)
                vs = ctx.op("scalar", "activation", probs[:, pc],
                            bk.ap[:, 0:NE], AF.Exp, bias=mv[:, 1:2], inc="sS")
                ctx.wait("vector", "sS", vs)
                ctx.op("vector", "reduce_sum", mv[:, 2:3], probs[:, pc],
                       axis=AX.X)
                ctx.op("vector", "reciprocal", mv[:, 3:4], mv[:, 2:3])
                vv2 = ctx.op("vector", "tensor_scalar", out=probs[:, pc],
                             in0=probs[:, pc], scalar1=mv[:, 3:4],
                             scalar2=None, op0=OP.mult, inc="sV")
                bk.freed = ("sV", vv2)
                perm = list(range(9 * j, 9 * j + 8))
                spare = 9 * j + 8
                for (i1, i2) in batcher8_pairs():
                    p1, p2 = perm[i1], perm[i2]
                    ctx.op("vector", "tensor_tensor",
                           out=probs[:, spare:spare + 1],
                           in0=probs[:, p1:p1 + 1], in1=probs[:, p2:p2 + 1],
                           op=OP.min)
                    ctx.op("vector", "tensor_tensor",
                           out=probs[:, p1:p1 + 1],
                           in0=probs[:, p1:p1 + 1], in1=probs[:, p2:p2 + 1],
                           op=OP.max)
                    perm[i2] = spare
                    spare = p2
                perms.append(perm)
            vsort = ctx.op("vector", "memset", ffa[:], 0.0, inc="sV")
            ctx.wait("tensor", "sV", vsort)
            ctx.wait("scalar", "sV", vsort)

            w1q = [None, None]
            w2q = [None, None]
            w1_used = [None, None]
            w2_used = [None, None]
            gel_guard = [None, None]

            def load_w1(e, din, slot):
                if w1_used[slot] is not None:
                    ctx.wait("sync", "sT", w1_used[slot])
                r = slice(128 * din, 128 * (din + 1))
                ctx.dma("sync", w1b[slot][:, 0:NI], w1hi_scr[e, r, :],
                        f"sW1{slot}")
                ctx.dma("sync", w1b[slot][:, NI:2 * NI], w1lo_scr[e, r, :],
                        f"sW1{slot}")
                w1q[slot] = True

            def load_w2(e, icg, slot):
                if w2_used[slot] is not None:
                    ctx.wait("sync", "sT", w2_used[slot])
                r = slice(512 * icg, 512 * (icg + 1))
                ctx.dma("sync",
                        w2b[slot][:, 0:4 * D].rearrange("p (c o) -> p c o", o=D),
                        w2hi_scr[e, r, :].rearrange("(c p) o -> p c o",
                                                    p=128), f"sW2{slot}")
                ctx.dma(
                    "sync",
                    w2b[slot][:, 4 * D:8 * D].rearrange("p (c o) -> p c o", o=D),
                    w2lo_scr[e, r, :].rearrange("(c p) o -> p c o", p=128),
                    f"sW2{slot}")
                w2q[slot] = True

            load_w1(0, 0, 0)
            for e in range(NE):
                for ip in range(4):
                    for din in range(8):
                        slot = din % 2
                        if din + 1 < 8:
                            load_w1(e, din + 1, (din + 1) % 2)
                        ctx.dwait("tensor", f"sW1{slot}")
                        for j in range(4):
                            for m in range(2):
                                ib = 2 * ip + m
                                bk = pb[2 * j + m]
                                if din == 0:
                                    ctx.waitp("tensor", bk.freed)
                                xs = slice(D * j + 128 * din,
                                           D * j + 128 * (din + 1))
                                last = (j == 3 and m == 1)
                                vt = mm3(ctx, bk.ap[:, :], xlnT2h[:, xs],
                                         xlnT2l[:, xs],
                                         w1b[slot][:, 512 * ib:512 * (ib + 1)],
                                         w1b[slot][:, NI + 512 * ib:
                                                    NI + 512 * (ib + 1)],
                                         start=(din == 0), stop=(din == 7),
                                         inc="sT" if last else None)
                        w1_used[slot] = vt
                        if din == 7:
                            for j in range(4):
                                for m in range(2):
                                    pb[2 * j + m].ready = ("sT", vt)
                    if ip < 3:
                        load_w1(e, 0, 0)
                    for j in range(4):
                        for m in range(2):
                            ib = 2 * ip + m
                            bk = pb[2 * j + m]
                            ctx.waitp("scalar", bk.ready)
                            if gel_guard[m] is not None:
                                ctx.wait("scalar", "sT", gel_guard[m])
                            vs = ctx.op("scalar", "activation",
                                        xln[m][:, 0:512], bk.ap[:, :],
                                        AF.Gelu, inc="sS")
                            ctx.wait("tensor", "sS", vs)
                            for q in range(4):
                                vt2 = ctx.op("tensor", "transpose",
                                             bk.ap[:, 128 * q:128 * (q + 1)],
                                             xln[m][:, 128 * q:128 * (q + 1)],
                                             ident[:],
                                             inc="sT" if q == 3 else None)
                            bk.ready = ("sT", vt2)
                            gel_guard[m] = vt2
                            ctx.waitp("vector", bk.ready)
                            dst = slice(NI * j + 512 * ib,
                                        NI * j + 512 * (ib + 1))
                            ctx.op("vector", "tensor_copy", gTh[:, dst],
                                   bk.ap[:, :])
                            vfree = ctx.op("vector", "tensor_tensor",
                                           out=gTl[:, dst], in0=bk.ap[:, :],
                                           in1=gTh[:, dst], op=OP.subtract,
                                           inc="sV")
                            bk.freed = ("sV", vfree)
                vgel = ctx.tot["sV"]
                ctx.wait("tensor", "sV", vgel)
                load_w2(e, 0, 0)
                for icg in range(8):
                    slot = icg % 2
                    if icg + 1 < 8:
                        load_w2(e, icg + 1, (icg + 1) % 2)
                    ctx.dwait("tensor", f"sW2{slot}")
                    for sub in range(4):
                        ic = 4 * icg + sub
                        for j in range(4):
                            gs = slice(NI * j + 128 * ic,
                                       NI * j + 128 * (ic + 1))
                            for hb in range(2):
                                bk = pb[2 * j + hb]
                                if ic == 0:
                                    ctx.waitp("tensor", bk.freed)
                                wsh = w2b[slot][:, D * sub + 512 * hb:
                                                D * sub + 512 * hb + 512]
                                wslo = w2b[slot][:, 4 * D + D * sub + 512 * hb:
                                                 4 * D + D * sub + 512 * hb + 512]
                                last = (sub == 3 and j == 3 and hb == 1)
                                vt = mm3(ctx, bk.ap[:, :], gTh[:, gs],
                                         gTl[:, gs], wsh, wslo,
                                         start=(ic == 0), stop=(ic == 31),
                                         inc="sT" if last else None)
                    w2_used[slot] = vt
                vlast = ctx.tot["sT"]
                for j in range(4):
                    for hb in range(2):
                        pb[2 * j + hb].ready = ("sT", vlast)
                for j in range(4):
                    pcol = perms[j][e]
                    for hb in range(2):
                        bk = pb[2 * j + hb]
                        ctx.waitp("vector", bk.ready)
                        dst = slice(D * j + 512 * hb, D * j + 512 * (hb + 1))
                        vv = ctx.op("vector", "scalar_tensor_tensor",
                                    out=ffa[:, dst], in0=bk.ap[:, :],
                                    scalar=probs[:, pcol:pcol + 1],
                                    in1=ffa[:, dst], op0=OP.mult, op1=OP.add,
                                    inc="sV")
                        bk.freed = ("sV", vv)
                if e + 1 < NE:
                    load_w1(e + 1, 0, 0)

            y_done = {}
            yadd_done = {}
            for j in range(4):
                t = 4 * sb + j
                sl = j % 2
                if (j - 2) in yadd_done:
                    ctx.wait("sync", "sV", yadd_done[j - 2])
                else:
                    ctx.wait("sync", "sS", norm_done[2 + sl])
                ctx.dma("sync", xb[sl][:],
                        xm_scr[128 * t:128 * (t + 1), :], f"sX{sl}")
                ctx.dwait("vector", f"sX{sl}")
                if (j - 2) in y_done:
                    ctx.dwait("vector", f"sY{sl}")
                vv = ctx.op("vector", "tensor_tensor", out=yb[sl][:],
                            in0=ffa[:, D * j:D * (j + 1)], in1=xb[sl][:],
                            op=OP.add, inc="sV")
                yadd_done[j] = vv
                ctx.wait("gpsimd", "sV", vv)
                ctx.dma("gpsimd", y[128 * t:128 * (t + 1), :], yb[sl][:],
                        f"sY{sl}")
                y_done[j] = True

        ctx.finish()
        p5s.close()
        return finalize(nc, ctx)


def finalize(nc, ctx):
    with nc.Block() as block:
        for ename in ENGINES:
            def make(ename):
                def prog(eng):
                    for fn in ctx.streams[ename]:
                        fn(eng)
                prog.__name__ = f"prog_{ename}"
                return prog
            getattr(block, ename)(make(ename))
    return nc


_NC_CACHE = {}


def get_program(S=8192, TOK=2048, debug=(), stages=99):
    key = (S, TOK, tuple(sorted(debug)), stages)
    if key not in _NC_CACHE:
        _NC_CACHE[key] = build_program(S, TOK, debug, stages)
    return _NC_CACHE[key]


def make_in_maps(inputs, n_cores=8, S=8192, TOK=2048):
    x = np.ascontiguousarray(np.asarray(inputs["L0_attn_x"], dtype=np.float32))
    B = x.shape[0]
    wqT = np.ascontiguousarray(np.asarray(inputs["L0_attn_Wq"]).T)
    wkT = np.ascontiguousarray(np.asarray(inputs["L0_attn_Wk"]).T)
    wvT = np.ascontiguousarray(np.asarray(inputs["L0_attn_Wv"]).T)
    wffT = np.ascontiguousarray(np.asarray(inputs["L0_attn_Wff"]).T)
    bff = np.ascontiguousarray(np.asarray(inputs["L0_attn_bff"]).reshape(1, D))
    wrT = np.ascontiguousarray(np.asarray(inputs["L0_ffn_Wr"]).T)
    w1T = np.ascontiguousarray(np.asarray(inputs["L0_ffn_W1"]).transpose(0, 2, 1))
    w2T = np.ascontiguousarray(np.asarray(inputs["L0_ffn_W2"]).transpose(0, 2, 1))
    ident = np.eye(128, dtype=np.float32)
    per_batch = n_cores // B
    in_maps = []
    for c in range(n_cores):
        b = c // per_batch
        q = c % per_batch
        xfb = np.ascontiguousarray(x[b])
        xob = np.ascontiguousarray(x[b, TOK * q:TOK * (q + 1)])
        in_maps.append({
            "xf": xfb, "xo": xob, "wqT": wqT, "wkT": wkT, "wvT": wvT,
            "wffT": wffT, "bff": bff, "wrT": wrT, "w1T": w1T, "w2T": w2T,
            "ident": ident,
        })
    return in_maps


def kernel(**inputs):
    S, TOK, NCORES = 8192, 2048, 8
    nc = get_program(S, TOK)
    in_maps = make_in_maps(inputs, NCORES, S, TOK)
    res = run_bass_kernel_spmd(nc, in_maps, core_ids=list(range(NCORES)))
    x = inputs["L0_attn_x"]
    B = x.shape[0]
    out = np.empty((B, S, D), dtype=np.float32)
    per_batch = NCORES // B
    for c in range(NCORES):
        b = c // per_batch
        q = c % per_batch
        out[b, TOK * q:TOK * (q + 1)] = res.results[c]["y"]
    return out


# version 19
# speedup vs baseline: 1.0337x; 1.0337x over previous
import contextlib
import numpy as np

import concourse.bass as bass
import concourse.mybir as mybir
from concourse.bass_utils import run_bass_kernel_spmd

F32 = mybir.dt.float32
F32R = mybir.dt.float32r
BF16 = mybir.dt.bfloat16
AF = mybir.ActivationFunctionType
OP = mybir.AluOpType
AX = mybir.AxisListType

ENGINES = ("sync", "gpsimd", "tensor", "vector", "scalar")

D = 1024
H = 16
DH = 64
NE = 8
NI = 4096
EPS = 1e-5


def batcher8_pairs():
    return [(0, 1), (2, 3), (4, 5), (6, 7),
            (0, 2), (1, 3), (4, 6), (5, 7),
            (1, 2), (5, 6),
            (0, 4), (1, 5), (2, 6), (3, 7),
            (2, 4), (3, 5),
            (1, 2), (3, 4), (5, 6)]


class Ctx:
    def __init__(self, nc, stack):
        self.nc = nc
        self.stack = stack
        self.streams = {e: [] for e in ENGINES}
        self._dirty = {}
        self.sems = {}
        self.tot = {}
        self.dma_sems = set()
        for name in ("sT", "sV", "sS", "sB"):
            self.sems[name] = stack.enter_context(nc.semaphore(name))
            self.tot[name] = 0

    def _get_sem(self, name):
        if name not in self.sems:
            self.sems[name] = self.stack.enter_context(self.nc.semaphore(name))
            self.tot[name] = 0
        return self.sems[name]

    def emit(self, engine, fn):
        self.streams[engine].append(fn)

    def op(self, engine, method, *args, inc=None, incby=1, **kwargs):
        h = self.sems[inc] if inc else None
        if engine in ("vector", "scalar"):
            if self._dirty.get(engine):
                self.emit(engine, lambda eng: eng.drain())
            self._dirty[engine] = method != "nop"

        def fn(eng, method=method, args=args, kwargs=kwargs, h=h, incby=incby):
            inst = getattr(eng, method)(*args, **kwargs)
            if h is not None:
                inst.then_inc(h, incby)

        self.emit(engine, fn)
        if inc:
            self.tot[inc] += incby
            return self.tot[inc]
        return None

    def wait(self, engine, sem, v):
        assert v is not None and v <= self.tot[sem], (engine, sem, v, self.tot[sem])
        if v <= 0:
            return
        h = self.sems[sem]
        self.emit(engine, lambda eng, h=h, v=v: eng.wait_ge(h, v))

    def waitp(self, engine, pair):
        if pair is not None:
            self.wait(engine, pair[0], pair[1])

    def dma(self, qeng, out, in_, sem):
        self._get_sem(sem)
        self.dma_sems.add(sem)
        return self.op(qeng, "dma_start", out=out, in_=in_, inc=sem, incby=16)

    def dwait(self, engine, sem):
        self._get_sem(sem)
        self.dma_sems.add(sem)
        self.wait(engine, sem, self.tot[sem])

    def barrier(self):
        for e in ENGINES:
            if e != "sync":
                self.emit(e, lambda eng: eng.drain())
                self._dirty[e] = False
            self.op(e, "nop", inc="sB")
        vb = self.tot["sB"]
        for e in ENGINES:
            self.wait(e, "sB", vb)
            for s in sorted(self.dma_sems):
                self.wait(e, s, self.tot[s])

    def finish(self):
        for e in ENGINES:
            for s in sorted(self.dma_sems):
                self.wait(e, s, self.tot[s])


class Bank:
    def __init__(self, ap):
        self.ap = ap
        self.ready = None
        self.freed = None


def mm3(ctx, ps_ap, shi, slo, mhi, mlo, start, stop, inc=None):
    ctx.op("tensor", "matmul", ps_ap, shi, mhi, start=start, stop=False)
    ctx.op("tensor", "matmul", ps_ap, shi, mlo, start=False, stop=False)
    return ctx.op("tensor", "matmul", ps_ap, slo, mhi, start=False, stop=stop,
                  inc=inc)


def build_program(S=8192, TOK=2048, debug=(), stages=99):
    nc = bass.Bass("TRN2", target_bir_lowering=False)
    NTF = S // 128
    NT = TOK // 128
    NQT = TOK // 512
    NKC = S // 128
    NKCH = NKC // 2
    NSB = TOK // 512
    assert TOK % 512 == 0 and S % 256 == 0 and NT <= NTF

    def dram_in(name, shape, dtype=F32):
        return nc.dram_tensor(name, shape, dtype, kind="ExternalInput").ap()

    def scratch(name, shape, dtype=F32):
        kind = "ExternalOutput" if name in debug else "Internal"
        return nc.dram_tensor(name, shape, dtype, kind=kind).ap()

    xf = dram_in("xf", [S, D])
    xo = dram_in("xo", [TOK, D])
    wqT = dram_in("wqT", [D, D])
    wkT = dram_in("wkT", [D, D])
    wvT = dram_in("wvT", [D, D])
    wffT = dram_in("wffT", [D, D])
    bffi = dram_in("bff", [1, D])
    wrT = dram_in("wrT", [D, NE])
    w1T = dram_in("w1T", [NE, D, NI])
    w2T = dram_in("w2T", [NE, NI, D])
    ident_in = dram_in("ident", [128, 128])

    y = nc.dram_tensor("y", [TOK, D], F32, kind="ExternalOutput").ap()

    kt_scr = scratch("kt_scr", [D, S])
    v_scr = scratch("v_scr", [S, D])
    qt_scr = scratch("qt_scr", [D, TOK])
    o2_scr = scratch("o2_scr", [NT, D, 128])
    xm_scr = scratch("xm_scr", [TOK, D])
    w1hi_scr = scratch("w1hi_scr", [NE, D, NI], BF16)
    w1lo_scr = scratch("w1lo_scr", [NE, D, NI], BF16)
    w2hi_scr = scratch("w2hi_scr", [NE, NI, D], BF16)
    w2lo_scr = scratch("w2lo_scr", [NE, NI, D], BF16)

    with contextlib.ExitStack() as stack:
        E = stack.enter_context
        ctx = Ctx(nc, stack)

        ident = E(nc.sbuf_tensor("identsb", [128, 128], F32))
        ones = E(nc.sbuf_tensor("ones", [1, 128], F32))
        bffb = E(nc.sbuf_tensor("bffb", [128, D], F32))
        a1 = E(nc.sbuf_tensor("a1", [128, NTF], F32))
        b1 = E(nc.sbuf_tensor("b1", [128, NTF], F32))
        a2 = E(nc.sbuf_tensor("a2", [128, NT], F32))
        b2 = E(nc.sbuf_tensor("b2", [128, NT], F32))
        wrh = E(nc.sbuf_tensor("wrh", [128, 8 * NE], BF16))
        wrl = E(nc.sbuf_tensor("wrl", [128, 8 * NE], BF16))
        xb = [E(nc.sbuf_tensor(f"xb{i}", [128, D], F32)) for i in range(2)]
        stat = E(nc.sbuf_tensor("stat", [128, 16], F32))
        mv = E(nc.sbuf_tensor("mv", [128, 8], F32))

        pb = [Bank(E(nc.psum_tensor(f"pb{i}", [128, 512], F32)))
              for i in range(8)]

        wsplit_jobs = []
        for e in range(NE):
            for r in range(D // 128):
                for cc in range(NI // 2048):
                    rs = slice(128 * r, 128 * (r + 1))
                    cs = slice(2048 * cc, 2048 * (cc + 1))
                    wsplit_jobs.append((w1T[e, rs, cs], w1hi_scr[e, rs, cs],
                                        w1lo_scr[e, rs, cs]))
            for r in range(NI // 256):
                rs = slice(256 * r, 256 * (r + 1))
                wsplit_jobs.append(
                    (w2T[e, rs, :].rearrange("(c p) o -> p c o", p=128),
                     w2hi_scr[e, rs, :].rearrange("(c p) o -> p c o", p=128),
                     w2lo_scr[e, rs, :].rearrange("(c p) o -> p c o", p=128)))
        ws_state = {"i": 0, "busy": [None, None]}

        def wsplit_step():
            i = ws_state["i"]
            if i >= len(wsplit_jobs):
                return
            ws_state["i"] = i + 1
            slot = i % 2
            src, dhi, dlo = wsplit_jobs[i]
            if ws_state["busy"][slot]:
                ctx.dwait("sync", f"sWS{slot}")
            v3 = len(src.shape) == 3
            win = (wps[slot][:].rearrange("p (c o) -> p c o", o=D)
                   if v3 else wps[slot][:])
            ctx.dma("sync", win, src, f"sWP{slot}")
            ctx.dwait("vector", f"sWP{slot}")
            ctx.op("vector", "tensor_copy", wph[slot][:], wps[slot][:])
            vl = ctx.op("vector", "tensor_tensor", out=wpl[slot][:],
                        in0=wps[slot][:], in1=wph[slot][:], op=OP.subtract,
                        inc="sV")
            ctx.wait("gpsimd", "sV", vl)
            hout = (wph[slot][:].rearrange("p (c o) -> p c o", o=D)
                    if v3 else wph[slot][:])
            lout = (wpl[slot][:].rearrange("p (c o) -> p c o", o=D)
                    if v3 else wpl[slot][:])
            ctx.dma("gpsimd", dhi, hout, f"sWS{slot}")
            ctx.dma("gpsimd", dlo, lout, f"sWS{slot}")
            ws_state["busy"][slot] = True

        def ln_stats(xb_ap, a_col, b_col):
            ctx.op("vector", "bn_stats", stat[:, 0:6], xb_ap[:, 0:512])
            ctx.op("vector", "bn_stats", stat[:, 6:12], xb_ap[:, 512:1024])
            ctx.op("vector", "bn_aggr", mv[:, 0:2], stat[:, 0:12])
            ctx.op("vector", "tensor_scalar", out=mv[:, 2:3], in0=mv[:, 1:2],
                   scalar1=EPS, scalar2=None, op0=OP.add)
            vv = ctx.op("vector", "reciprocal", mv[:, 3:4], mv[:, 2:3],
                        inc="sV")
            ctx.wait("scalar", "sV", vv)
            vs = ctx.op("scalar", "activation", a_col, mv[:, 3:4], AF.Sqrt,
                        inc="sS")
            ctx.wait("vector", "sS", vs)
            return ctx.op("vector", "scalar_tensor_tensor", out=b_col,
                          in0=mv[:, 0:1], scalar=-1.0, in1=a_col,
                          op0=OP.mult, op1=OP.mult, inc="sV")

        def transpose8(src_sb, banks, dst_f32, wait_tensor=()):
            for sem, v in wait_tensor:
                ctx.wait("tensor", sem, v)
            for g in range(2):
                bk = banks[g]
                ctx.waitp("tensor", bk.freed)
                for q in range(4):
                    c = g * 4 + q
                    vt = ctx.op("tensor", "transpose",
                                bk.ap[:, 128 * q:128 * (q + 1)],
                                src_sb[:, 128 * c:128 * (c + 1)], ident[:],
                                inc="sT" if q == 3 else None)
                bk.ready = ("sT", vt)
            vv = None
            for g in range(2):
                bk = banks[g]
                ctx.waitp("vector", bk.ready)
                vv = ctx.op("vector", "tensor_copy",
                            dst_f32[:, 512 * g:512 * (g + 1)], bk.ap[:],
                            inc="sV")
                bk.freed = ("sV", vv)
            return vt, vv

        def split_f32r(dst_hi, dst_lo, src, n):
            ctx.op("vector", "tensor_copy", dst_hi[:, 0:n], src[:, 0:n])
            return ctx.op("vector", "tensor_tensor", out=dst_lo[:, 0:n],
                          in0=src[:, 0:n], in1=dst_hi[:, 0:n],
                          op=OP.subtract, inc="sV")

        ctx.dma("sync", ident[:], ident_in, "sIN")
        ctx.dma("sync", xb[0][0:1, :], bffi, "sIN")
        ctx.dma("sync",
                xb[1][:, 0:8 * NE].rearrange("p (c e) -> p c e", e=NE),
                wrT.rearrange("(c p) e -> p c e", p=128), "sIN")
        ctx.op("vector", "memset", ones[:], 1.0)
        ctx.dwait("vector", "sIN")
        ctx.op("vector", "tensor_copy", wrh[:, 0:8 * NE], xb[1][:, 0:8 * NE])
        ctx.op("vector", "tensor_tensor", out=wrl[:, 0:8 * NE],
               in0=xb[1][:, 0:8 * NE], in1=wrh[:, 0:8 * NE], op=OP.subtract)
        vv0 = ctx.op("vector", "nop", inc="sV")
        ctx.dwait("tensor", "sIN")
        ctx.wait("tensor", "sV", vv0)
        vt0 = None
        for g in range(2):
            vt0 = ctx.op("tensor", "matmul", pb[6 + g].ap[:, :], ones[0:1, :],
                         xb[0][0:1, 512 * g:512 * (g + 1)], start=True,
                         stop=True, inc="sT")
        ctx.wait("vector", "sT", vt0)
        for g in range(2):
            ctx.op("vector", "tensor_copy", bffb[:, 512 * g:512 * (g + 1)],
                   pb[6 + g].ap[:])
        ctx.op("vector", "nop", inc="sV")
        ctx.barrier()

        if stages < 1:
            ctx.finish()
            return finalize(nc, ctx)

        xln = [E(nc.sbuf_tensor(f"xln{i}", [128, D], F32)) for i in range(2)]
        p2s = contextlib.ExitStack()
        E2 = p2s.enter_context
        wWh = E2(nc.sbuf_tensor("wWh", [128, 8 * D], BF16))
        wWl = E2(nc.sbuf_tensor("wWl", [128, 8 * D], BF16))
        wraw = E2(nc.sbuf_tensor("wraw", [128, 8 * D], F32))
        wps = [E2(nc.sbuf_tensor(f"wps{i}", [128, 2048], F32)) for i in range(2)]
        wph = [E2(nc.sbuf_tensor(f"wph{i}", [128, 2048], BF16)) for i in range(2)]
        wpl = [E2(nc.sbuf_tensor(f"wpl{i}", [128, 2048], BF16)) for i in range(2)]
        xlT = [E2(nc.sbuf_tensor(f"xlT{i}", [128, D], F32)) for i in range(2)]
        xlh = [E2(nc.sbuf_tensor(f"xlh{i}", [128, D], BF16)) for i in range(2)]
        xll = [E2(nc.sbuf_tensor(f"xll{i}", [128, D], BF16)) for i in range(2)]
        ko = [E2(nc.sbuf_tensor(f"ko{i}", [128, D], F32)) for i in range(2)]
        koT = [E2(nc.sbuf_tensor(f"koT{i}", [128, D], F32)) for i in range(2)]

        def load_weight_T(wT):
            ctx.dma("sync",
                    wraw[:].rearrange("p (c o) -> p c o", o=D),
                    wT.rearrange("(c p) o -> p c o", p=128), "sIN")
            ctx.dwait("vector", "sIN")
            return split_f32r(wWh, wWl, wraw, 8 * D)

        def proj_phase(kind):
            wT = {"K": wkT, "V": wvT, "Q": wqT}[kind]
            ntiles = NTF if kind in ("K", "V") else NT
            xsrc = xf if kind in ("K", "V") else xo
            vw = load_weight_T(wT)
            ctx.wait("tensor", "sV", vw)
            last_ev = {}
            store_done = {}
            tdone = {}

            def stage_a(t):
                sl = t % 2
                if (t - 2) in last_ev:
                    ctx.wait("sync", "sS", last_ev[t - 2])
                ctx.dma("sync", xb[sl][:], xsrc[128 * t:128 * (t + 1), :],
                        f"sX{sl}")
                ctx.dwait("vector", f"sX{sl}")
                if kind in ("K", "Q"):
                    vb = ln_stats(xb[sl][:], a1[:, t:t + 1], b1[:, t:t + 1])
                    ctx.wait("scalar", "sV", vb)
                ctx.dwait("scalar", f"sX{sl}")
                if (t - 2) in tdone:
                    ctx.wait("scalar", "sT", tdone[t - 2])
                last_ev[t] = ctx.op("scalar", "activation", xln[sl][:],
                                    xb[sl][:], AF.Identity,
                                    bias=b1[:, t:t + 1], scale=a1[:, t:t + 1],
                                    inc="sS")

            def stage_kt(t):
                sl = t % 2
                if (t - 2) in store_done:
                    ctx.dwait("vector", f"sST{sl}")
                _, vvk = transpose8(ko[sl][:], [pb[4], pb[5]], koT[sl][:],
                                    wait_tensor=[("sS", evdone[t])])
                ctx.wait("gpsimd", "sV", vvk)
                dst = kt_scr if kind == "K" else qt_scr
                for c in range(8):
                    ctx.dma("gpsimd",
                            dst[128 * c:128 * (c + 1), 128 * t:128 * (t + 1)],
                            koT[sl][:, 128 * c:128 * (c + 1)], f"sST{sl}")
                store_done[t] = True

            evdone = {}
            spl = {}
            stage_a(0)
            tdone[0], _ = transpose8(xln[0][:], [pb[6], pb[7]], xlT[0][:],
                                     wait_tensor=[("sS", last_ev[0])])
            spl[0] = split_f32r(xlh[0], xll[0], xlT[0][:], D)
            for t in range(ntiles):
                sl = t % 2
                if t + 1 < ntiles:
                    nsl = (t + 1) % 2
                    stage_a(t + 1)
                    tdone[t + 1], _ = transpose8(
                        xln[nsl][:], [pb[6], pb[7]], xlT[nsl][:],
                        wait_tensor=[("sS", last_ev[t + 1])])
                    spl[t + 1] = split_f32r(xlh[nsl], xll[nsl], xlT[nsl][:],
                                            D)
                ctx.wait("tensor", "sV", spl[t])
                for blk in range(2):
                    bk = pb[blk]
                    ctx.waitp("tensor", bk.freed)
                    for din in range(8):
                        vt = mm3(
                            ctx, bk.ap[:, :],
                            xlh[sl][:, 128 * din:128 * (din + 1)],
                            xll[sl][:, 128 * din:128 * (din + 1)],
                            wWh[:, D * din + 512 * blk:D * din + 512 * blk + 512],
                            wWl[:, D * din + 512 * blk:D * din + 512 * blk + 512],
                            start=(din == 0), stop=(din == 7),
                            inc="sT" if din == 7 else None)
                    bk.ready = ("sT", vt)
                if kind != "V" and t >= 1:
                    stage_kt(t - 1)
                if (t - 2) in store_done or (kind == "V" and (t - 2) in evdone):
                    ctx.dwait("scalar", f"sST{sl}")
                for blk in range(2):
                    bk = pb[blk]
                    ctx.waitp("scalar", bk.ready)
                    sc = (1.0 / 32.0) if kind == "Q" else 1.0
                    vs = ctx.op("scalar", "activation",
                                ko[sl][:, 512 * blk:512 * (blk + 1)],
                                bk.ap[:], AF.Copy, scale=sc, inc="sS")
                    bk.freed = ("sS", vs)
                evdone[t] = ctx.tot["sS"]
                if kind == "V":
                    ctx.wait("gpsimd", "sS", evdone[t])
                    ctx.dma("gpsimd", v_scr[128 * t:128 * (t + 1), :],
                            ko[sl][:], f"sST{sl}")
                    store_done[t] = True
                wsplit_step()
            if kind != "V":
                stage_kt(ntiles - 1)

        proj_phase("K")
        ctx.barrier()
        proj_phase("V")
        ctx.barrier()
        proj_phase("Q")
        while ws_state["i"] < len(wsplit_jobs):
            wsplit_step()
        ctx.barrier()
        p2s.close()

        if stages < 2:
            ctx.finish()
            return finalize(nc, ctx)

        SH = S // 2
        p3s = contextlib.ExitStack()
        E3 = p3s.enter_context
        kraw = [E3(nc.sbuf_tensor(f"kraw{i}", [128, SH], F32))
                for i in range(2)]
        kstk = [E3(nc.sbuf_tensor(f"kstk{i}", [128, SH], BF16))
                for i in range(2)]
        vraw = [E3(nc.sbuf_tensor(f"vraw{i}", [128, NKCH * 64], F32))
                for i in range(2)]
        vst1 = [E3(nc.sbuf_tensor(f"vst1{i}", [128, NKCH * 65], BF16))
                for i in range(2)]
        vst2 = [E3(nc.sbuf_tensor(f"vst2{i}", [128, NKCH * 64], BF16))
                for i in range(2)]
        qst = [E3(nc.sbuf_tensor(f"qst{i}", [128, TOK], F32)) for i in range(2)]
        qhi = [E3(nc.sbuf_tensor(f"qhi{i}", [128, TOK], BF16))
               for i in range(2)]
        qlo = [E3(nc.sbuf_tensor(f"qlo{i}", [128, TOK], BF16))
               for i in range(2)]
        Eb = [E3(nc.sbuf_tensor(f"Eb{i}", [128, 2048], F32))
              for i in range(3)]
        EhiB = [E3(nc.sbuf_tensor(f"Ehi{i}", [128, 2048], BF16))
                for i in range(3)]
        EloB = [E3(nc.sbuf_tensor(f"Elo{i}", [128, 2048], BF16))
                for i in range(3)]
        rrow = E3(nc.sbuf_tensor("rrow", [1, 512], F32))
        bcb = E3(nc.sbuf_tensor("bcb", [64, 512], F32))
        o2st = [E3(nc.sbuf_tensor(f"o2st{i}", [128, 512], F32))
                for i in range(2)]

        for i in range(2):
            ctx.op("vector", "memset",
                   vst1[i][:].rearrange("p (k c) -> p k c", c=65)
                   [:, :, 64:65], 1.0)
        vv_ones = ctx.op("vector", "nop", inc="sV")
        ctx.wait("gpsimd", "sV", vv_ones)
        ctx.wait("tensor", "sV", vv_ones)
        ctx._get_sem("sP")

        def gop(method, *a, **k):
            ctx.wait("gpsimd", "sP", ctx.tot["sP"])
            return ctx.op("gpsimd", method, *a, inc="sP", **k)

        kv_freed = [None, None]
        q_freed = [None, None]
        kv_ready = [None, None]
        q_ready = [None, None]

        def load_head_kv(h, half, slot):
            if kv_freed[slot] is not None:
                ctx.wait("sync", *kv_freed[slot])
            src = kt_scr[64 * h:64 * (h + 1), SH * half:SH * (half + 1)]
            ctx.dma("sync", kraw[slot][0:64, :], src, f"sKV{slot}")
            ctx.dma("sync", kraw[slot][64:128, :], src, f"sKV{slot}")
            G = 4
            for g in range(NKCH // G):
                rows = v_scr[SH * half + 128 * G * g:
                             SH * half + 128 * G * (g + 1),
                             64 * h:64 * (h + 1)]
                ctx.dma("sync",
                        vraw[slot][:].rearrange("p (k c) -> p k c", c=64)
                        [:, G * g:G * (g + 1), :],
                        rows.rearrange("(k p) c -> p k c", p=128),
                        f"sKV{slot}")
            ctx.dwait("gpsimd", f"sKV{slot}")
            gop("tensor_copy", kstk[slot][0:64, :], kraw[slot][0:64, :])
            gop("tensor_copy", kstk[slot][64:128, :], kraw[slot][64:128, :])
            gop("tensor_tensor", out=kstk[slot][64:128, :],
                in0=kraw[slot][64:128, :], in1=kstk[slot][64:128, :],
                op=OP.subtract)
            vv3 = vraw[slot][:].rearrange("p (k c) -> p k c", c=64)
            vh3 = vst1[slot][:].rearrange("p (k c) -> p k c", c=65)[:, :, 0:64]
            vl3 = vst2[slot][:].rearrange("p (k c) -> p k c", c=64)
            gop("tensor_copy", vh3, vv3)
            vv = gop("tensor_tensor", out=vl3, in0=vv3, in1=vh3,
                     op=OP.subtract)
            kv_ready[slot] = ("sP", vv)

        def load_head_q(h, slot):
            if q_freed[slot] is not None:
                ctx.wait("sync", *q_freed[slot])
            src = qt_scr[64 * h:64 * (h + 1), :]
            ctx.dma("sync", qst[slot][0:64, :], src, f"sQL{slot}")
            ctx.dma("sync", qst[slot][64:128, :], src, f"sQL{slot}")
            ctx.dwait("vector", f"sQL{slot}")
            ctx.op("vector", "tensor_copy", qhi[slot][:], qst[slot][:])
            vv = ctx.op("vector", "tensor_tensor", out=qlo[slot][:],
                        in0=qst[slot][:], in1=qhi[slot][:], op=OP.subtract,
                        inc="sV")
            q_ready[slot] = ("sV", vv)

        load_head_q(0, 0)
        load_head_kv(0, 0, 0)
        e_free2 = [None, None, None]
        eb_users = [[], [], []]
        ehilo_v = [None, None, None]

        for h in range(H):
            qslot = h % 2
            if h + 1 < H:
                load_head_q(h + 1, (h + 1) % 2)
            ctx.waitp("tensor", q_ready[qslot])
            for qt in range(NQT):
                ctx.waitp("tensor", pb[qt].freed)
            last_mm = None
            pipe = []
            for kci in range(NKC + 2):
                if kci < NKC:
                    half = kci // NKCH
                    kc = kci % NKCH
                    slot = (2 * h + half) % 2
                    par = kci % 3
                    if kc == 0:
                        ctx.waitp("tensor", kv_ready[slot])
                    if kc == 2:
                        if half == 0:
                            load_head_kv(h, 1, slot ^ 1)
                        elif h + 1 < H:
                            load_head_kv(h + 1, 0, slot ^ 1)
                    kap = kstk[slot][:, 128 * kc:128 * (kc + 1)]
                    for qt in range(NQT):
                        sbk = pb[4 + qt]
                        ctx.waitp("tensor", sbk.freed)
                        ctx.op("tensor", "matmul", sbk.ap[:, :], kap,
                               qhi[qslot][:, 512 * qt:512 * (qt + 1)],
                               start=True, stop=False)
                        vt = ctx.op("tensor", "matmul", sbk.ap[:, :], kap,
                                    qlo[qslot][:, 512 * qt:512 * (qt + 1)],
                                    start=False, stop=True, inc="sT")
                        sbk.ready = ("sT", vt)
                    for pr in eb_users[par]:
                        ctx.waitp("scalar", pr)
                    for qt in range(NQT):
                        sbk = pb[4 + qt]
                        ctx.waitp("scalar", sbk.ready)
                        vexp = ctx.op("scalar", "activation",
                                      Eb[par][:, 512 * qt:512 * (qt + 1)],
                                      sbk.ap[:], AF.Exp, inc="sS")
                        sbk.freed = ("sS", vexp)
                    if e_free2[par] is not None:
                        ctx.wait("scalar", "sT", e_free2[par])
                    vhi = ctx.op("scalar", "activation",
                                 EhiB[par][:, 0:512 * NQT],
                                 Eb[par][:, 0:512 * NQT], AF.Copy, inc="sS")
                    ctx.wait("vector", "sS", vhi)
                    if e_free2[par] is not None:
                        ctx.wait("vector", "sT", e_free2[par])
                    vlo = ctx.op("vector", "scalar_tensor_tensor",
                                 out=EloB[par][:, 0:512 * NQT],
                                 in0=Eb[par][:, 0:512 * NQT],
                                 scalar=1.0, in1=EhiB[par][:, 0:512 * NQT],
                                 op0=OP.mult, op1=OP.subtract, inc="sV")
                    ehilo_v[par] = (vhi, vlo)
                    eb_users[par] = [("sS", vhi), ("sV", vlo)]
                    pipe.append((vst1[slot][:, 65 * kc:65 * kc + 65],
                                 vst2[slot][:, 64 * kc:64 * kc + 64],
                                 par, kci == 0, kci == NKC - 1,
                                 slot if kc == NKCH - 1 else None))
                if kci >= 2:
                    pvap1, pvap2, ppar, pfirst, plast, freeslot = pipe.pop(0)
                    ctx.wait("tensor", "sS", ehilo_v[ppar][0])
                    ctx.wait("tensor", "sV", ehilo_v[ppar][1])
                    for qt in range(NQT):
                        qs = slice(512 * qt, 512 * (qt + 1))
                        ctx.op("tensor", "matmul", pb[qt].ap[0:65, :], pvap1,
                               EhiB[ppar][:, qs], start=pfirst, stop=False)
                        ctx.op("tensor", "matmul", pb[qt].ap[0:64, :], pvap2,
                               EhiB[ppar][:, qs], start=False, stop=False)
                        vt2 = ctx.op("tensor", "matmul", pb[qt].ap[0:65, :],
                                     pvap1, EloB[ppar][:, qs],
                                     start=False, stop=plast, inc="sT")
                        if plast:
                            pb[qt].ready = ("sT", vt2)
                        last_mm = vt2
                    e_free2[ppar] = last_mm
                    if freeslot is not None:
                        kv_freed[freeslot] = ("sT", last_mm)
            q_freed[qslot] = ("sT", last_mm)
            for qt in range(NQT):
                bk = pb[qt]
                par = qt % 2
                ctx.waitp("vector", bk.ready)
                vvr = ctx.op("vector", "reciprocal", rrow[:],
                             bk.ap[64:65, :], inc="sV")
                sbc = pb[6 + par] if NQT == 4 else pb[5]
                ctx.waitp("tensor", sbc.freed)
                ctx.wait("tensor", "sV", vvr)
                vt2 = ctx.op("tensor", "matmul", sbc.ap[0:64, :],
                             ones[0:1, 0:64], rrow[:], start=True, stop=True,
                             inc="sT")
                sbc.ready = ("sT", vt2)
                ctx.wait("vector", "sT", vt2)
                ctx.op("vector", "tensor_copy", bcb[:], sbc.ap[0:64, :])
                ctx.dwait("vector", f"sO2{par}")
                vdv = ctx.op("vector", "tensor_tensor",
                             out=o2st[par][0:64, :],
                             in0=bk.ap[0:64, :], in1=bcb[:], op=OP.mult,
                             inc="sV")
                bk.freed = ("sV", vdv)
                sbc.freed = ("sV", vdv)
                ctx.wait("gpsimd", "sV", vdv)
                for j in range(4):
                    tt = 4 * qt + j
                    ctx.dma("gpsimd", o2_scr[tt, 64 * h:64 * (h + 1), :],
                            o2st[par][0:64, 128 * j:128 * (j + 1)],
                            f"sO2{par}")
        ctx.barrier()
        p3s.close()

        if stages < 3:
            ctx.finish()
            return finalize(nc, ctx)

        p4s = contextlib.ExitStack()
        E4 = p4s.enter_context
        wWh = E4(nc.sbuf_tensor("wWh4", [128, 8 * D], BF16))
        wWl = E4(nc.sbuf_tensor("wWl4", [128, 8 * D], BF16))
        wraw = E4(nc.sbuf_tensor("wraw4", [128, 8 * D], F32))
        o2r = [E4(nc.sbuf_tensor(f"o2r{i}", [128, D], F32)) for i in range(2)]
        o2h = E4(nc.sbuf_tensor("o2h", [128, D], BF16))
        o2l = E4(nc.sbuf_tensor("o2l", [128, D], BF16))
        xmid = [E4(nc.sbuf_tensor(f"xmid{i}", [128, D], F32)) for i in range(2)]
        tacc = E4(nc.sbuf_tensor("tacc", [128, D], F32))

        def load_weight_ff(wT):
            ctx.dma("sync",
                    wraw[:].rearrange("p (c o) -> p c o", o=D),
                    wT.rearrange("(c p) o -> p c o", p=128), "sIN")
            ctx.dwait("vector", "sIN")
            return split_f32r(wWh, wWl, wraw, 8 * D)

        vw = load_weight_ff(wffT)
        ctx.wait("tensor", "sV", vw)
        ctx.dma("sync", o2r[0][:].rearrange("p (c t) -> p c t", t=128),
                o2_scr[0].rearrange("(c p) t -> p c t", p=128), "sOR0")
        ctx.dma("sync", xb[0][:], xo[0:128, :], "sX0")
        prev_mm = None
        split_done = {}
        xm_used = {}
        xm_stored = {}
        for t in range(NT):
            sl = t % 2
            if t + 1 < NT:
                if (t - 1) in split_done:
                    ctx.wait("sync", "sV", split_done[t - 1])
                if (t - 1) in xm_used:
                    ctx.wait("sync", "sV", xm_used[t - 1])
                ctx.dma(
                    "sync",
                    o2r[(t + 1) % 2][:].rearrange("p (c t) -> p c t", t=128),
                    o2_scr[t + 1].rearrange("(c p) t -> p c t", p=128),
                    f"sOR{(t + 1) % 2}")
                ctx.dma("sync", xb[(t + 1) % 2][:],
                        xo[128 * (t + 1):128 * (t + 2), :],
                        f"sX{(t + 1) % 2}")
            ctx.dwait("vector", f"sOR{sl}")
            if prev_mm is not None:
                ctx.wait("vector", "sT", prev_mm)
            vsp = split_f32r(o2h, o2l, o2r[sl][:], D)
            split_done[t] = vsp
            ctx.wait("tensor", "sV", vsp)
            for blk in range(2):
                bk = pb[blk]
                ctx.waitp("tensor", bk.freed)
                for hc in range(8):
                    vt = mm3(
                        ctx, bk.ap[:, :],
                        o2h[:, 128 * hc:128 * (hc + 1)],
                        o2l[:, 128 * hc:128 * (hc + 1)],
                        wWh[:, D * hc + 512 * blk:D * hc + 512 * blk + 512],
                        wWl[:, D * hc + 512 * blk:D * hc + 512 * blk + 512],
                        start=(hc == 0), stop=(hc == 7),
                        inc="sT" if hc == 7 else None)
                bk.ready = ("sT", vt)
            prev_mm = ctx.tot["sT"]
            ctx.dwait("vector", f"sX{sl}")
            if (t - 2) in xm_stored:
                ctx.dwait("vector", f"sXM{sl}")
            for blk in range(2):
                bk = pb[blk]
                ctx.waitp("vector", bk.ready)
                cs = slice(512 * blk, 512 * (blk + 1))
                ctx.op("vector", "tensor_tensor", out=tacc[:, cs],
                       in0=bk.ap[:, :], in1=bffb[:, cs], op=OP.add)
                vv = ctx.op("vector", "tensor_tensor", out=xmid[sl][:, cs],
                            in0=tacc[:, cs], in1=xb[sl][:, cs], op=OP.add,
                            inc="sV")
                bk.freed = ("sV", vv)
            xm_used[t] = ctx.tot["sV"]
            vb2 = ln_stats(xmid[sl][:], a2[:, t:t + 1], b2[:, t:t + 1])
            ctx.wait("gpsimd", "sV", vb2)
            ctx.dma("gpsimd", xm_scr[128 * t:128 * (t + 1), :],
                    xmid[sl][:], f"sXM{sl}")
            xm_stored[t] = True
        ctx.barrier()
        p4s.close()

        if stages < 4:
            ctx.finish()
            return finalize(nc, ctx)

        p5s = contextlib.ExitStack()
        E5 = p5s.enter_context
        xlnT2h = E5(nc.sbuf_tensor("xlnT2h", [128, 4 * D], BF16))
        xlnT2l = E5(nc.sbuf_tensor("xlnT2l", [128, 4 * D], BF16))
        gTh = E5(nc.sbuf_tensor("gTh", [128, 4 * NI], BF16))
        gTl = E5(nc.sbuf_tensor("gTl", [128, 4 * NI], BF16))
        w1b = [E5(nc.sbuf_tensor(f"w1b{i}", [128, 2 * NI], BF16))
               for i in range(2)]
        w2b = [E5(nc.sbuf_tensor(f"w2b{i}", [128, 8 * D], BF16))
               for i in range(2)]
        ffa = E5(nc.sbuf_tensor("ffa", [128, 4 * D], F32))
        probs = E5(nc.sbuf_tensor("probs", [128, 4 * 9], F32))
        yb = [E5(nc.sbuf_tensor(f"yb{i}", [128, D], F32)) for i in range(2)]

        for sb in range(NSB):
            ctx.barrier()
            xln_used = {}
            norm_done = {}
            for j in range(4):
                t = 4 * sb + j
                sl = j % 2
                if (j - 2) in norm_done:
                    ctx.wait("sync", "sS", norm_done[j - 2])
                ctx.dma("sync", xb[sl][:],
                        xm_scr[128 * t:128 * (t + 1), :], f"sX{sl}")
                ctx.dwait("scalar", f"sX{sl}")
                if (j - 2) in xln_used:
                    ctx.wait("scalar", "sT", xln_used[j - 2])
                vs = ctx.op("scalar", "activation", xln[sl][:], xb[sl][:],
                            AF.Identity, bias=b2[:, t:t + 1],
                            scale=a2[:, t:t + 1], inc="sS")
                norm_done[j] = vs
                ctx.wait("tensor", "sS", vs)
                for g in range(2):
                    bk = pb[6 + g]
                    ctx.waitp("tensor", bk.freed)
                    for q in range(4):
                        c = 4 * g + q
                        vt = ctx.op("tensor", "transpose",
                                    bk.ap[:, 128 * q:128 * (q + 1)],
                                    xln[sl][:, 128 * c:128 * (c + 1)],
                                    ident[:], inc="sT" if q == 3 else None)
                    bk.ready = ("sT", vt)
                xln_used[j] = ctx.tot["sT"]
                for g in range(2):
                    bk = pb[6 + g]
                    ctx.waitp("vector", bk.ready)
                    dst = slice(D * j + 512 * g, D * j + 512 * (g + 1))
                    ctx.op("vector", "tensor_copy", xlnT2h[:, dst],
                           bk.ap[:, :])
                    vfree = ctx.op("vector", "tensor_tensor",
                                   out=xlnT2l[:, dst], in0=bk.ap[:, :],
                                   in1=xlnT2h[:, dst], op=OP.subtract,
                                   inc="sV")
                    bk.freed = ("sV", vfree)
            vxt = ctx.tot["sV"]

            ctx.wait("tensor", "sV", vxt)
            for j in range(4):
                bk = pb[j]
                ctx.waitp("tensor", bk.freed)
                for din in range(8):
                    xs = slice(D * j + 128 * din, D * j + 128 * (din + 1))
                    vt = mm3(ctx, bk.ap[:, 0:NE], xlnT2h[:, xs],
                             xlnT2l[:, xs],
                             wrh[:, NE * din:NE * (din + 1)],
                             wrl[:, NE * din:NE * (din + 1)],
                             start=(din == 0), stop=(din == 7),
                             inc="sT" if din == 7 else None)
                bk.ready = ("sT", vt)
            perms = []
            for j in range(4):
                bk = pb[j]
                ctx.waitp("vector", bk.ready)
                ctx.op("vector", "reduce_max", mv[:, 0:1], bk.ap[:, 0:NE],
                       axis=AX.X)
                vv = ctx.op("vector", "tensor_scalar", out=mv[:, 1:2],
                            in0=mv[:, 0:1], scalar1=-1.0, scalar2=None,
                            op0=OP.mult, inc="sV")
                ctx.wait("scalar", "sV", vv)
                pc = slice(9 * j, 9 * j + 8)
                vs = ctx.op("scalar", "activation", probs[:, pc],
                            bk.ap[:, 0:NE], AF.Exp, bias=mv[:, 1:2], inc="sS")
                ctx.wait("vector", "sS", vs)
                ctx.op("vector", "reduce_sum", mv[:, 2:3], probs[:, pc],
                       axis=AX.X)
                ctx.op("vector", "reciprocal", mv[:, 3:4], mv[:, 2:3])
                vv2 = ctx.op("vector", "tensor_scalar", out=probs[:, pc],
                             in0=probs[:, pc], scalar1=mv[:, 3:4],
                             scalar2=None, op0=OP.mult, inc="sV")
                bk.freed = ("sV", vv2)
                perm = list(range(9 * j, 9 * j + 8))
                spare = 9 * j + 8
                for (i1, i2) in batcher8_pairs():
                    p1, p2 = perm[i1], perm[i2]
                    ctx.op("vector", "tensor_tensor",
                           out=probs[:, spare:spare + 1],
                           in0=probs[:, p1:p1 + 1], in1=probs[:, p2:p2 + 1],
                           op=OP.min)
                    ctx.op("vector", "tensor_tensor",
                           out=probs[:, p1:p1 + 1],
                           in0=probs[:, p1:p1 + 1], in1=probs[:, p2:p2 + 1],
                           op=OP.max)
                    perm[i2] = spare
                    spare = p2
                perms.append(perm)
            vsort = ctx.op("vector", "memset", ffa[:], 0.0, inc="sV")
            ctx.wait("tensor", "sV", vsort)
            ctx.wait("scalar", "sV", vsort)

            w1q = [None, None]
            w2q = [None, None]
            w1_used = [None, None]
            w2_used = [None, None]
            gel_guard = [None, None]

            def load_w1(e, din, slot):
                if w1_used[slot] is not None:
                    ctx.wait("sync", "sT", w1_used[slot])
                r = slice(128 * din, 128 * (din + 1))
                ctx.dma("sync", w1b[slot][:, 0:NI], w1hi_scr[e, r, :],
                        f"sW1{slot}")
                ctx.dma("sync", w1b[slot][:, NI:2 * NI], w1lo_scr[e, r, :],
                        f"sW1{slot}")
                w1q[slot] = True

            def load_w2(e, icg, slot):
                if w2_used[slot] is not None:
                    ctx.wait("sync", "sT", w2_used[slot])
                r = slice(512 * icg, 512 * (icg + 1))
                ctx.dma("sync",
                        w2b[slot][:, 0:4 * D].rearrange("p (c o) -> p c o", o=D),
                        w2hi_scr[e, r, :].rearrange("(c p) o -> p c o",
                                                    p=128), f"sW2{slot}")
                ctx.dma(
                    "sync",
                    w2b[slot][:, 4 * D:8 * D].rearrange("p (c o) -> p c o", o=D),
                    w2lo_scr[e, r, :].rearrange("(c p) o -> p c o", p=128),
                    f"sW2{slot}")
                w2q[slot] = True

            load_w1(0, 0, 0)
            for e in range(NE):
                for ip in range(4):
                    for din in range(8):
                        slot = din % 2
                        if din + 1 < 8:
                            load_w1(e, din + 1, (din + 1) % 2)
                        ctx.dwait("tensor", f"sW1{slot}")
                        for j in range(4):
                            for m in range(2):
                                ib = 2 * ip + m
                                bk = pb[2 * j + m]
                                if din == 0:
                                    ctx.waitp("tensor", bk.freed)
                                xs = slice(D * j + 128 * din,
                                           D * j + 128 * (din + 1))
                                last = (j == 3 and m == 1)
                                vt = mm3(ctx, bk.ap[:, :], xlnT2h[:, xs],
                                         xlnT2l[:, xs],
                                         w1b[slot][:, 512 * ib:512 * (ib + 1)],
                                         w1b[slot][:, NI + 512 * ib:
                                                    NI + 512 * (ib + 1)],
                                         start=(din == 0), stop=(din == 7),
                                         inc="sT" if (last or din == 7)
                                         else None)
                                if din == 7:
                                    bk.ready = ("sT", vt)
                        w1_used[slot] = vt
                    if ip < 3:
                        load_w1(e, 0, 0)
                    for j in range(4):
                        for m in range(2):
                            ib = 2 * ip + m
                            bk = pb[2 * j + m]
                            ctx.waitp("scalar", bk.ready)
                            if gel_guard[m] is not None:
                                ctx.wait("scalar", "sT", gel_guard[m])
                            vs = ctx.op("scalar", "activation",
                                        xln[m][:, 0:512], bk.ap[:, :],
                                        AF.Gelu, inc="sS")
                            ctx.wait("tensor", "sS", vs)
                            for q in range(4):
                                vt2 = ctx.op("tensor", "transpose",
                                             bk.ap[:, 128 * q:128 * (q + 1)],
                                             xln[m][:, 128 * q:128 * (q + 1)],
                                             ident[:],
                                             inc="sT" if q == 3 else None)
                            bk.ready = ("sT", vt2)
                            gel_guard[m] = vt2
                            ctx.waitp("vector", bk.ready)
                            dst = slice(NI * j + 512 * ib,
                                        NI * j + 512 * (ib + 1))
                            ctx.op("vector", "tensor_copy", gTh[:, dst],
                                   bk.ap[:, :])
                            vfree = ctx.op("vector", "tensor_tensor",
                                           out=gTl[:, dst], in0=bk.ap[:, :],
                                           in1=gTh[:, dst], op=OP.subtract,
                                           inc="sV")
                            bk.freed = ("sV", vfree)
                vgel = ctx.tot["sV"]
                ctx.wait("tensor", "sV", vgel)
                load_w2(e, 0, 0)
                for icg in range(8):
                    slot = icg % 2
                    if icg + 1 < 8:
                        load_w2(e, icg + 1, (icg + 1) % 2)
                    ctx.dwait("tensor", f"sW2{slot}")
                    for sub in range(4):
                        ic = 4 * icg + sub
                        for j in range(4):
                            gs = slice(NI * j + 128 * ic,
                                       NI * j + 128 * (ic + 1))
                            for hb in range(2):
                                bk = pb[2 * j + hb]
                                if ic == 0:
                                    ctx.waitp("tensor", bk.freed)
                                wsh = w2b[slot][:, D * sub + 512 * hb:
                                                D * sub + 512 * hb + 512]
                                wslo = w2b[slot][:, 4 * D + D * sub + 512 * hb:
                                                 4 * D + D * sub + 512 * hb + 512]
                                last = (sub == 3 and j == 3 and hb == 1)
                                vt = mm3(ctx, bk.ap[:, :], gTh[:, gs],
                                         gTl[:, gs], wsh, wslo,
                                         start=(ic == 0), stop=(ic == 31),
                                         inc="sT" if (last or ic == 31)
                                         else None)
                                if ic == 31:
                                    bk.ready = ("sT", vt)
                    w2_used[slot] = vt
                for j in range(4):
                    pcol = perms[j][e]
                    for hb in range(2):
                        bk = pb[2 * j + hb]
                        ctx.waitp("vector", bk.ready)
                        dst = slice(D * j + 512 * hb, D * j + 512 * (hb + 1))
                        vv = ctx.op("vector", "scalar_tensor_tensor",
                                    out=ffa[:, dst], in0=bk.ap[:, :],
                                    scalar=probs[:, pcol:pcol + 1],
                                    in1=ffa[:, dst], op0=OP.mult, op1=OP.add,
                                    inc="sV")
                        bk.freed = ("sV", vv)
                if e + 1 < NE:
                    load_w1(e + 1, 0, 0)

            y_done = {}
            yadd_done = {}
            for j in range(4):
                t = 4 * sb + j
                sl = j % 2
                if (j - 2) in yadd_done:
                    ctx.wait("sync", "sV", yadd_done[j - 2])
                else:
                    ctx.wait("sync", "sS", norm_done[2 + sl])
                ctx.dma("sync", xb[sl][:],
                        xm_scr[128 * t:128 * (t + 1), :], f"sX{sl}")
                ctx.dwait("vector", f"sX{sl}")
                if (j - 2) in y_done:
                    ctx.dwait("vector", f"sY{sl}")
                vv = ctx.op("vector", "tensor_tensor", out=yb[sl][:],
                            in0=ffa[:, D * j:D * (j + 1)], in1=xb[sl][:],
                            op=OP.add, inc="sV")
                yadd_done[j] = vv
                ctx.wait("gpsimd", "sV", vv)
                ctx.dma("gpsimd", y[128 * t:128 * (t + 1), :], yb[sl][:],
                        f"sY{sl}")
                y_done[j] = True

        ctx.finish()
        p5s.close()
        return finalize(nc, ctx)


def finalize(nc, ctx):
    with nc.Block() as block:
        for ename in ENGINES:
            def make(ename):
                def prog(eng):
                    for fn in ctx.streams[ename]:
                        fn(eng)
                prog.__name__ = f"prog_{ename}"
                return prog
            getattr(block, ename)(make(ename))
    return nc


_NC_CACHE = {}


def get_program(S=8192, TOK=2048, debug=(), stages=99):
    key = (S, TOK, tuple(sorted(debug)), stages)
    if key not in _NC_CACHE:
        _NC_CACHE[key] = build_program(S, TOK, debug, stages)
    return _NC_CACHE[key]


def make_in_maps(inputs, n_cores=8, S=8192, TOK=2048):
    x = np.ascontiguousarray(np.asarray(inputs["L0_attn_x"], dtype=np.float32))
    B = x.shape[0]
    wqT = np.ascontiguousarray(np.asarray(inputs["L0_attn_Wq"]).T)
    wkT = np.ascontiguousarray(np.asarray(inputs["L0_attn_Wk"]).T)
    wvT = np.ascontiguousarray(np.asarray(inputs["L0_attn_Wv"]).T)
    wffT = np.ascontiguousarray(np.asarray(inputs["L0_attn_Wff"]).T)
    bff = np.ascontiguousarray(np.asarray(inputs["L0_attn_bff"]).reshape(1, D))
    wrT = np.ascontiguousarray(np.asarray(inputs["L0_ffn_Wr"]).T)
    w1T = np.ascontiguousarray(np.asarray(inputs["L0_ffn_W1"]).transpose(0, 2, 1))
    w2T = np.ascontiguousarray(np.asarray(inputs["L0_ffn_W2"]).transpose(0, 2, 1))
    ident = np.eye(128, dtype=np.float32)
    per_batch = n_cores // B
    in_maps = []
    for c in range(n_cores):
        b = c // per_batch
        q = c % per_batch
        xfb = np.ascontiguousarray(x[b])
        xob = np.ascontiguousarray(x[b, TOK * q:TOK * (q + 1)])
        in_maps.append({
            "xf": xfb, "xo": xob, "wqT": wqT, "wkT": wkT, "wvT": wvT,
            "wffT": wffT, "bff": bff, "wrT": wrT, "w1T": w1T, "w2T": w2T,
            "ident": ident,
        })
    return in_maps


def kernel(**inputs):
    S, TOK, NCORES = 8192, 2048, 8
    nc = get_program(S, TOK)
    in_maps = make_in_maps(inputs, NCORES, S, TOK)
    res = run_bass_kernel_spmd(nc, in_maps, core_ids=list(range(NCORES)))
    x = inputs["L0_attn_x"]
    B = x.shape[0]
    out = np.empty((B, S, D), dtype=np.float32)
    per_batch = NCORES // B
    for c in range(NCORES):
        b = c // per_batch
        q = c % per_batch
        out[b, TOK * q:TOK * (q + 1)] = res.results[c]["y"]
    return out


# version 20
# speedup vs baseline: 1.0404x; 1.0065x over previous
import contextlib
import numpy as np

import concourse.bass as bass
import concourse.mybir as mybir
from concourse.bass_utils import run_bass_kernel_spmd

F32 = mybir.dt.float32
F32R = mybir.dt.float32r
BF16 = mybir.dt.bfloat16
AF = mybir.ActivationFunctionType
OP = mybir.AluOpType
AX = mybir.AxisListType

ENGINES = ("sync", "gpsimd", "tensor", "vector", "scalar")

D = 1024
H = 16
DH = 64
NE = 8
NI = 4096
EPS = 1e-5


def batcher8_pairs():
    return [(0, 1), (2, 3), (4, 5), (6, 7),
            (0, 2), (1, 3), (4, 6), (5, 7),
            (1, 2), (5, 6),
            (0, 4), (1, 5), (2, 6), (3, 7),
            (2, 4), (3, 5),
            (1, 2), (3, 4), (5, 6)]


class Ctx:
    def __init__(self, nc, stack):
        self.nc = nc
        self.stack = stack
        self.streams = {e: [] for e in ENGINES}
        self._dirty = {}
        self.sems = {}
        self.tot = {}
        self.dma_sems = set()
        for name in ("sT", "sV", "sS", "sB"):
            self.sems[name] = stack.enter_context(nc.semaphore(name))
            self.tot[name] = 0

    def _get_sem(self, name):
        if name not in self.sems:
            self.sems[name] = self.stack.enter_context(self.nc.semaphore(name))
            self.tot[name] = 0
        return self.sems[name]

    def emit(self, engine, fn):
        self.streams[engine].append(fn)

    def op(self, engine, method, *args, inc=None, incby=1, **kwargs):
        h = self.sems[inc] if inc else None
        if engine in ("vector", "scalar"):
            if self._dirty.get(engine):
                self.emit(engine, lambda eng: eng.drain())
            self._dirty[engine] = method != "nop"

        def fn(eng, method=method, args=args, kwargs=kwargs, h=h, incby=incby):
            inst = getattr(eng, method)(*args, **kwargs)
            if h is not None:
                inst.then_inc(h, incby)

        self.emit(engine, fn)
        if inc:
            self.tot[inc] += incby
            return self.tot[inc]
        return None

    def wait(self, engine, sem, v):
        assert v is not None and v <= self.tot[sem], (engine, sem, v, self.tot[sem])
        if v <= 0:
            return
        h = self.sems[sem]
        self.emit(engine, lambda eng, h=h, v=v: eng.wait_ge(h, v))

    def waitp(self, engine, pair):
        if pair is not None:
            self.wait(engine, pair[0], pair[1])

    def dma(self, qeng, out, in_, sem):
        self._get_sem(sem)
        self.dma_sems.add(sem)
        return self.op(qeng, "dma_start", out=out, in_=in_, inc=sem, incby=16)

    def dwait(self, engine, sem):
        self._get_sem(sem)
        self.dma_sems.add(sem)
        self.wait(engine, sem, self.tot[sem])

    def barrier(self):
        for e in ENGINES:
            if e != "sync":
                self.emit(e, lambda eng: eng.drain())
                self._dirty[e] = False
            self.op(e, "nop", inc="sB")
        vb = self.tot["sB"]
        for e in ENGINES:
            self.wait(e, "sB", vb)
            for s in sorted(self.dma_sems):
                self.wait(e, s, self.tot[s])

    def finish(self):
        for e in ENGINES:
            for s in sorted(self.dma_sems):
                self.wait(e, s, self.tot[s])


class Bank:
    def __init__(self, ap):
        self.ap = ap
        self.ready = None
        self.freed = None


def mm3(ctx, ps_ap, shi, slo, mhi, mlo, start, stop, inc=None):
    ctx.op("tensor", "matmul", ps_ap, shi, mhi, start=start, stop=False)
    ctx.op("tensor", "matmul", ps_ap, shi, mlo, start=False, stop=False)
    return ctx.op("tensor", "matmul", ps_ap, slo, mhi, start=False, stop=stop,
                  inc=inc)


def build_program(S=8192, TOK=2048, debug=(), stages=99):
    nc = bass.Bass("TRN2", target_bir_lowering=False)
    NTF = S // 128
    NT = TOK // 128
    NQT = TOK // 512
    NKC = S // 128
    NKCH = NKC // 2
    NSB = TOK // 512
    assert TOK % 512 == 0 and S % 256 == 0 and NT <= NTF

    def dram_in(name, shape, dtype=F32):
        return nc.dram_tensor(name, shape, dtype, kind="ExternalInput").ap()

    def scratch(name, shape, dtype=F32):
        kind = "ExternalOutput" if name in debug else "Internal"
        return nc.dram_tensor(name, shape, dtype, kind=kind).ap()

    xf = dram_in("xf", [S, D])
    xo = dram_in("xo", [TOK, D])
    wqT = dram_in("wqT", [D, D])
    wkT = dram_in("wkT", [D, D])
    wvT = dram_in("wvT", [D, D])
    wffT = dram_in("wffT", [D, D])
    bffi = dram_in("bff", [1, D])
    wrT = dram_in("wrT", [D, NE])
    w1T = dram_in("w1T", [NE, D, NI])
    w2T = dram_in("w2T", [NE, NI, D])
    ident_in = dram_in("ident", [128, 128])

    y = nc.dram_tensor("y", [TOK, D], F32, kind="ExternalOutput").ap()

    kt_scr = scratch("kt_scr", [D, S])
    v_scr = scratch("v_scr", [S, D])
    qt_scr = scratch("qt_scr", [D, TOK])
    o2_scr = scratch("o2_scr", [NT, D, 128])
    xm_scr = scratch("xm_scr", [TOK, D])
    w1hi_scr = scratch("w1hi_scr", [NE, D, NI], BF16)
    w1lo_scr = scratch("w1lo_scr", [NE, D, NI], BF16)
    w2hi_scr = scratch("w2hi_scr", [NE, NI, D], BF16)
    w2lo_scr = scratch("w2lo_scr", [NE, NI, D], BF16)

    with contextlib.ExitStack() as stack:
        E = stack.enter_context
        ctx = Ctx(nc, stack)

        ident = E(nc.sbuf_tensor("identsb", [128, 128], F32))
        ones = E(nc.sbuf_tensor("ones", [1, 128], F32))
        bffb = E(nc.sbuf_tensor("bffb", [128, D], F32))
        a1 = E(nc.sbuf_tensor("a1", [128, NTF], F32))
        b1 = E(nc.sbuf_tensor("b1", [128, NTF], F32))
        a2 = E(nc.sbuf_tensor("a2", [128, NT], F32))
        b2 = E(nc.sbuf_tensor("b2", [128, NT], F32))
        wrh = E(nc.sbuf_tensor("wrh", [128, 8 * NE], BF16))
        wrl = E(nc.sbuf_tensor("wrl", [128, 8 * NE], BF16))
        xb = [E(nc.sbuf_tensor(f"xb{i}", [128, D], F32)) for i in range(2)]
        stat = E(nc.sbuf_tensor("stat", [128, 16], F32))
        mv = E(nc.sbuf_tensor("mv", [128, 8], F32))

        pb = [Bank(E(nc.psum_tensor(f"pb{i}", [128, 512], F32)))
              for i in range(8)]

        wsplit_jobs = []
        for e in range(NE):
            for r in range(D // 128):
                for cc in range(NI // 1024):
                    rs = slice(128 * r, 128 * (r + 1))
                    cs = slice(1024 * cc, 1024 * (cc + 1))
                    wsplit_jobs.append((w1T[e, rs, cs], w1hi_scr[e, rs, cs],
                                        w1lo_scr[e, rs, cs]))
            for r in range(NI // 128):
                rs = slice(128 * r, 128 * (r + 1))
                wsplit_jobs.append((w2T[e, rs, :], w2hi_scr[e, rs, :],
                                    w2lo_scr[e, rs, :]))
        ws_state = {"i": 0, "busy": [None, None]}

        def wsplit_step():
            i = ws_state["i"]
            if i >= len(wsplit_jobs):
                return
            ws_state["i"] = i + 1
            slot = i % 2
            srcap, dhi, dlo = wsplit_jobs[i]
            if ws_state["busy"][slot]:
                ctx.dwait("sync", f"sWS{slot}")
            ctx.dma("sync", wps[slot][:], srcap, f"sWP{slot}")
            ctx.dwait("vector", f"sWP{slot}")
            ctx.op("vector", "tensor_copy", wph[slot][:], wps[slot][:])
            vl = ctx.op("vector", "tensor_tensor", out=wpl[slot][:],
                        in0=wps[slot][:], in1=wph[slot][:], op=OP.subtract,
                        inc="sV")
            ctx.wait("gpsimd", "sV", vl)
            ctx.dma("gpsimd", dhi, wph[slot][:], f"sWS{slot}")
            ctx.dma("gpsimd", dlo, wpl[slot][:], f"sWS{slot}")
            ws_state["busy"][slot] = True

        def ln_stats(xb_ap, a_col, b_col):
            ctx.op("vector", "bn_stats", stat[:, 0:6], xb_ap[:, 0:512])
            ctx.op("vector", "bn_stats", stat[:, 6:12], xb_ap[:, 512:1024])
            ctx.op("vector", "bn_aggr", mv[:, 0:2], stat[:, 0:12])
            ctx.op("vector", "tensor_scalar", out=mv[:, 2:3], in0=mv[:, 1:2],
                   scalar1=EPS, scalar2=None, op0=OP.add)
            vv = ctx.op("vector", "reciprocal", mv[:, 3:4], mv[:, 2:3],
                        inc="sV")
            ctx.wait("scalar", "sV", vv)
            vs = ctx.op("scalar", "activation", a_col, mv[:, 3:4], AF.Sqrt,
                        inc="sS")
            ctx.wait("vector", "sS", vs)
            return ctx.op("vector", "scalar_tensor_tensor", out=b_col,
                          in0=mv[:, 0:1], scalar=-1.0, in1=a_col,
                          op0=OP.mult, op1=OP.mult, inc="sV")

        def transpose8(src_sb, banks, dst_f32, wait_tensor=()):
            for sem, v in wait_tensor:
                ctx.wait("tensor", sem, v)
            for g in range(2):
                bk = banks[g]
                ctx.waitp("tensor", bk.freed)
                for q in range(4):
                    c = g * 4 + q
                    vt = ctx.op("tensor", "transpose",
                                bk.ap[:, 128 * q:128 * (q + 1)],
                                src_sb[:, 128 * c:128 * (c + 1)], ident[:],
                                inc="sT" if q == 3 else None)
                bk.ready = ("sT", vt)
            vv = None
            for g in range(2):
                bk = banks[g]
                ctx.waitp("vector", bk.ready)
                vv = ctx.op("vector", "tensor_copy",
                            dst_f32[:, 512 * g:512 * (g + 1)], bk.ap[:],
                            inc="sV")
                bk.freed = ("sV", vv)
            return vt, vv

        def split_f32r(dst_hi, dst_lo, src, n):
            ctx.op("vector", "tensor_copy", dst_hi[:, 0:n], src[:, 0:n])
            return ctx.op("vector", "tensor_tensor", out=dst_lo[:, 0:n],
                          in0=src[:, 0:n], in1=dst_hi[:, 0:n],
                          op=OP.subtract, inc="sV")

        ctx.dma("sync", ident[:], ident_in, "sIN")
        ctx.dma("sync", xb[0][0:1, :], bffi, "sIN")
        ctx.dma("sync",
                xb[1][:, 0:8 * NE].rearrange("p (c e) -> p c e", e=NE),
                wrT.rearrange("(c p) e -> p c e", p=128), "sIN")
        ctx.op("vector", "memset", ones[:], 1.0)
        ctx.dwait("vector", "sIN")
        ctx.op("vector", "tensor_copy", wrh[:, 0:8 * NE], xb[1][:, 0:8 * NE])
        ctx.op("vector", "tensor_tensor", out=wrl[:, 0:8 * NE],
               in0=xb[1][:, 0:8 * NE], in1=wrh[:, 0:8 * NE], op=OP.subtract)
        vv0 = ctx.op("vector", "nop", inc="sV")
        ctx.dwait("tensor", "sIN")
        ctx.wait("tensor", "sV", vv0)
        vt0 = None
        for g in range(2):
            vt0 = ctx.op("tensor", "matmul", pb[6 + g].ap[:, :], ones[0:1, :],
                         xb[0][0:1, 512 * g:512 * (g + 1)], start=True,
                         stop=True, inc="sT")
        ctx.wait("vector", "sT", vt0)
        for g in range(2):
            ctx.op("vector", "tensor_copy", bffb[:, 512 * g:512 * (g + 1)],
                   pb[6 + g].ap[:])
        ctx.op("vector", "nop", inc="sV")
        ctx.barrier()

        if stages < 1:
            ctx.finish()
            return finalize(nc, ctx)

        xln = [E(nc.sbuf_tensor(f"xln{i}", [128, D], F32)) for i in range(2)]
        wsps = contextlib.ExitStack()
        EW = wsps.enter_context
        wps = [EW(nc.sbuf_tensor(f"wps{i}", [128, 1024], F32)) for i in range(2)]
        wph = [EW(nc.sbuf_tensor(f"wph{i}", [128, 1024], BF16)) for i in range(2)]
        wpl = [EW(nc.sbuf_tensor(f"wpl{i}", [128, 1024], BF16)) for i in range(2)]
        p2s = contextlib.ExitStack()
        E2 = p2s.enter_context
        wWh = E2(nc.sbuf_tensor("wWh", [128, 8 * D], BF16))
        wWl = E2(nc.sbuf_tensor("wWl", [128, 8 * D], BF16))
        wraw = E2(nc.sbuf_tensor("wraw", [128, 8 * D], F32))
        xlT = [E2(nc.sbuf_tensor(f"xlT{i}", [128, D], F32)) for i in range(2)]
        xlh = [E2(nc.sbuf_tensor(f"xlh{i}", [128, D], BF16)) for i in range(2)]
        xll = [E2(nc.sbuf_tensor(f"xll{i}", [128, D], BF16)) for i in range(2)]
        ko = [E2(nc.sbuf_tensor(f"ko{i}", [128, D], F32)) for i in range(2)]
        koT = [E2(nc.sbuf_tensor(f"koT{i}", [128, D], F32)) for i in range(2)]

        def load_weight_T(wT):
            ctx.dma("sync",
                    wraw[:].rearrange("p (c o) -> p c o", o=D),
                    wT.rearrange("(c p) o -> p c o", p=128), "sIN")
            ctx.dwait("vector", "sIN")
            return split_f32r(wWh, wWl, wraw, 8 * D)

        def proj_phase(kind):
            wT = {"K": wkT, "V": wvT, "Q": wqT}[kind]
            ntiles = NTF if kind in ("K", "V") else NT
            xsrc = xf if kind in ("K", "V") else xo
            vw = load_weight_T(wT)
            ctx.wait("tensor", "sV", vw)
            last_ev = {}
            store_done = {}
            tdone = {}

            def stage_a(t):
                sl = t % 2
                if (t - 2) in last_ev:
                    ctx.wait("sync", "sS", last_ev[t - 2])
                ctx.dma("sync", xb[sl][:], xsrc[128 * t:128 * (t + 1), :],
                        f"sX{sl}")
                ctx.dwait("vector", f"sX{sl}")
                if kind in ("K", "Q"):
                    vb = ln_stats(xb[sl][:], a1[:, t:t + 1], b1[:, t:t + 1])
                    ctx.wait("scalar", "sV", vb)
                ctx.dwait("scalar", f"sX{sl}")
                if (t - 2) in tdone:
                    ctx.wait("scalar", "sT", tdone[t - 2])
                last_ev[t] = ctx.op("scalar", "activation", xln[sl][:],
                                    xb[sl][:], AF.Identity,
                                    bias=b1[:, t:t + 1], scale=a1[:, t:t + 1],
                                    inc="sS")

            def stage_kt(t):
                sl = t % 2
                if (t - 2) in store_done:
                    ctx.dwait("vector", f"sST{sl}")
                _, vvk = transpose8(ko[sl][:], [pb[4], pb[5]], koT[sl][:],
                                    wait_tensor=[("sS", evdone[t])])
                ctx.wait("gpsimd", "sV", vvk)
                dst = kt_scr if kind == "K" else qt_scr
                for c in range(8):
                    ctx.dma("gpsimd",
                            dst[128 * c:128 * (c + 1), 128 * t:128 * (t + 1)],
                            koT[sl][:, 128 * c:128 * (c + 1)], f"sST{sl}")
                store_done[t] = True

            evdone = {}
            spl = {}
            stage_a(0)
            tdone[0], _ = transpose8(xln[0][:], [pb[6], pb[7]], xlT[0][:],
                                     wait_tensor=[("sS", last_ev[0])])
            spl[0] = split_f32r(xlh[0], xll[0], xlT[0][:], D)
            for t in range(ntiles):
                sl = t % 2
                if t + 1 < ntiles:
                    nsl = (t + 1) % 2
                    stage_a(t + 1)
                    tdone[t + 1], _ = transpose8(
                        xln[nsl][:], [pb[6], pb[7]], xlT[nsl][:],
                        wait_tensor=[("sS", last_ev[t + 1])])
                    spl[t + 1] = split_f32r(xlh[nsl], xll[nsl], xlT[nsl][:],
                                            D)
                ctx.wait("tensor", "sV", spl[t])
                for blk in range(2):
                    bk = pb[blk]
                    ctx.waitp("tensor", bk.freed)
                    for din in range(8):
                        vt = mm3(
                            ctx, bk.ap[:, :],
                            xlh[sl][:, 128 * din:128 * (din + 1)],
                            xll[sl][:, 128 * din:128 * (din + 1)],
                            wWh[:, D * din + 512 * blk:D * din + 512 * blk + 512],
                            wWl[:, D * din + 512 * blk:D * din + 512 * blk + 512],
                            start=(din == 0), stop=(din == 7),
                            inc="sT" if din == 7 else None)
                    bk.ready = ("sT", vt)
                if kind != "V" and t >= 1:
                    stage_kt(t - 1)
                if (t - 2) in store_done or (kind == "V" and (t - 2) in evdone):
                    ctx.dwait("scalar", f"sST{sl}")
                for blk in range(2):
                    bk = pb[blk]
                    ctx.waitp("scalar", bk.ready)
                    sc = (1.0 / 32.0) if kind == "Q" else 1.0
                    vs = ctx.op("scalar", "activation",
                                ko[sl][:, 512 * blk:512 * (blk + 1)],
                                bk.ap[:], AF.Copy, scale=sc, inc="sS")
                    bk.freed = ("sS", vs)
                evdone[t] = ctx.tot["sS"]
                if kind == "V":
                    ctx.wait("gpsimd", "sS", evdone[t])
                    ctx.dma("gpsimd", v_scr[128 * t:128 * (t + 1), :],
                            ko[sl][:], f"sST{sl}")
                    store_done[t] = True
                wsplit_step()
                if kind in ("K", "V"):
                    wsplit_step()
            if kind != "V":
                stage_kt(ntiles - 1)

        proj_phase("K")
        ctx.barrier()
        proj_phase("V")
        ctx.barrier()
        proj_phase("Q")
        ctx.barrier()
        p2s.close()

        if stages < 2:
            ctx.finish()
            return finalize(nc, ctx)

        SH = S // 2
        p3s = contextlib.ExitStack()
        E3 = p3s.enter_context
        kraw = [E3(nc.sbuf_tensor(f"kraw{i}", [128, SH], F32))
                for i in range(2)]
        kstk = [E3(nc.sbuf_tensor(f"kstk{i}", [128, SH], BF16))
                for i in range(2)]
        vraw = [E3(nc.sbuf_tensor(f"vraw{i}", [128, NKCH * 64], F32))
                for i in range(2)]
        vst1 = [E3(nc.sbuf_tensor(f"vst1{i}", [128, NKCH * 65], BF16))
                for i in range(2)]
        vst2 = [E3(nc.sbuf_tensor(f"vst2{i}", [128, NKCH * 64], BF16))
                for i in range(2)]
        qst = [E3(nc.sbuf_tensor(f"qst{i}", [128, TOK], F32)) for i in range(2)]
        qhi = [E3(nc.sbuf_tensor(f"qhi{i}", [128, TOK], BF16))
               for i in range(2)]
        qlo = [E3(nc.sbuf_tensor(f"qlo{i}", [128, TOK], BF16))
               for i in range(2)]
        Eb = [E3(nc.sbuf_tensor(f"Eb{i}", [128, 2048], F32))
              for i in range(3)]
        EhiB = [E3(nc.sbuf_tensor(f"Ehi{i}", [128, 2048], BF16))
                for i in range(3)]
        EloB = [E3(nc.sbuf_tensor(f"Elo{i}", [128, 2048], BF16))
                for i in range(3)]
        rrow = E3(nc.sbuf_tensor("rrow", [1, 512], F32))
        bcb = E3(nc.sbuf_tensor("bcb", [64, 512], F32))
        o2st = [E3(nc.sbuf_tensor(f"o2st{i}", [128, 512], F32))
                for i in range(2)]

        for i in range(2):
            ctx.op("vector", "memset",
                   vst1[i][:].rearrange("p (k c) -> p k c", c=65)
                   [:, :, 64:65], 1.0)
        vv_ones = ctx.op("vector", "nop", inc="sV")
        ctx.wait("gpsimd", "sV", vv_ones)
        ctx.wait("tensor", "sV", vv_ones)
        ctx._get_sem("sP")

        def gop(method, *a, **k):
            ctx.wait("gpsimd", "sP", ctx.tot["sP"])
            return ctx.op("gpsimd", method, *a, inc="sP", **k)

        kv_freed = [None, None]
        q_freed = [None, None]
        kv_ready = [None, None]
        q_ready = [None, None]

        def load_head_kv(h, half, slot):
            if kv_freed[slot] is not None:
                ctx.wait("sync", *kv_freed[slot])
            src = kt_scr[64 * h:64 * (h + 1), SH * half:SH * (half + 1)]
            ctx.dma("sync", kraw[slot][0:64, :], src, f"sKV{slot}")
            ctx.dma("sync", kraw[slot][64:128, :], src, f"sKV{slot}")
            G = 4
            for g in range(NKCH // G):
                rows = v_scr[SH * half + 128 * G * g:
                             SH * half + 128 * G * (g + 1),
                             64 * h:64 * (h + 1)]
                ctx.dma("sync",
                        vraw[slot][:].rearrange("p (k c) -> p k c", c=64)
                        [:, G * g:G * (g + 1), :],
                        rows.rearrange("(k p) c -> p k c", p=128),
                        f"sKV{slot}")
            ctx.dwait("gpsimd", f"sKV{slot}")
            gop("tensor_copy", kstk[slot][0:64, :], kraw[slot][0:64, :])
            gop("tensor_copy", kstk[slot][64:128, :], kraw[slot][64:128, :])
            gop("tensor_tensor", out=kstk[slot][64:128, :],
                in0=kraw[slot][64:128, :], in1=kstk[slot][64:128, :],
                op=OP.subtract)
            vv3 = vraw[slot][:].rearrange("p (k c) -> p k c", c=64)
            vh3 = vst1[slot][:].rearrange("p (k c) -> p k c", c=65)[:, :, 0:64]
            vl3 = vst2[slot][:].rearrange("p (k c) -> p k c", c=64)
            gop("tensor_copy", vh3, vv3)
            vv = gop("tensor_tensor", out=vl3, in0=vv3, in1=vh3,
                     op=OP.subtract)
            kv_ready[slot] = ("sP", vv)

        def load_head_q(h, slot):
            if q_freed[slot] is not None:
                ctx.wait("sync", *q_freed[slot])
            src = qt_scr[64 * h:64 * (h + 1), :]
            ctx.dma("sync", qst[slot][0:64, :], src, f"sQL{slot}")
            ctx.dma("sync", qst[slot][64:128, :], src, f"sQL{slot}")
            ctx.dwait("vector", f"sQL{slot}")
            ctx.op("vector", "tensor_copy", qhi[slot][:], qst[slot][:])
            vv = ctx.op("vector", "tensor_tensor", out=qlo[slot][:],
                        in0=qst[slot][:], in1=qhi[slot][:], op=OP.subtract,
                        inc="sV")
            q_ready[slot] = ("sV", vv)

        load_head_q(0, 0)
        load_head_kv(0, 0, 0)
        e_free2 = [None, None, None]
        eb_users = [[], [], []]
        ehilo_v = [None, None, None]

        for h in range(H):
            qslot = h % 2
            if h + 1 < H:
                load_head_q(h + 1, (h + 1) % 2)
            ctx.waitp("tensor", q_ready[qslot])
            for qt in range(NQT):
                ctx.waitp("tensor", pb[qt].freed)
            last_mm = None
            pipe = []
            for kci in range(NKC + 2):
                if kci < NKC:
                    half = kci // NKCH
                    kc = kci % NKCH
                    slot = (2 * h + half) % 2
                    par = kci % 3
                    if kc == 0:
                        ctx.waitp("tensor", kv_ready[slot])
                    if kc == 2:
                        if half == 0:
                            load_head_kv(h, 1, slot ^ 1)
                        elif h + 1 < H:
                            load_head_kv(h + 1, 0, slot ^ 1)
                        for _ in range(4):
                            wsplit_step()
                    kap = kstk[slot][:, 128 * kc:128 * (kc + 1)]
                    for qt in range(NQT):
                        sbk = pb[4 + qt]
                        ctx.waitp("tensor", sbk.freed)
                        ctx.op("tensor", "matmul", sbk.ap[:, :], kap,
                               qhi[qslot][:, 512 * qt:512 * (qt + 1)],
                               start=True, stop=False)
                        vt = ctx.op("tensor", "matmul", sbk.ap[:, :], kap,
                                    qlo[qslot][:, 512 * qt:512 * (qt + 1)],
                                    start=False, stop=True, inc="sT")
                        sbk.ready = ("sT", vt)
                    for pr in eb_users[par]:
                        ctx.waitp("scalar", pr)
                    for qt in range(NQT):
                        sbk = pb[4 + qt]
                        ctx.waitp("scalar", sbk.ready)
                        vexp = ctx.op("scalar", "activation",
                                      Eb[par][:, 512 * qt:512 * (qt + 1)],
                                      sbk.ap[:], AF.Exp, inc="sS")
                        sbk.freed = ("sS", vexp)
                    if e_free2[par] is not None:
                        ctx.wait("scalar", "sT", e_free2[par])
                    vhi = ctx.op("scalar", "activation",
                                 EhiB[par][:, 0:512 * NQT],
                                 Eb[par][:, 0:512 * NQT], AF.Copy, inc="sS")
                    ctx.wait("vector", "sS", vhi)
                    if e_free2[par] is not None:
                        ctx.wait("vector", "sT", e_free2[par])
                    vlo = ctx.op("vector", "scalar_tensor_tensor",
                                 out=EloB[par][:, 0:512 * NQT],
                                 in0=Eb[par][:, 0:512 * NQT],
                                 scalar=1.0, in1=EhiB[par][:, 0:512 * NQT],
                                 op0=OP.mult, op1=OP.subtract, inc="sV")
                    ehilo_v[par] = (vhi, vlo)
                    eb_users[par] = [("sS", vhi), ("sV", vlo)]
                    pipe.append((vst1[slot][:, 65 * kc:65 * kc + 65],
                                 vst2[slot][:, 64 * kc:64 * kc + 64],
                                 par, kci == 0, kci == NKC - 1,
                                 slot if kc == NKCH - 1 else None))
                if kci >= 2:
                    pvap1, pvap2, ppar, pfirst, plast, freeslot = pipe.pop(0)
                    ctx.wait("tensor", "sS", ehilo_v[ppar][0])
                    ctx.wait("tensor", "sV", ehilo_v[ppar][1])
                    for qt in range(NQT):
                        qs = slice(512 * qt, 512 * (qt + 1))
                        ctx.op("tensor", "matmul", pb[qt].ap[0:65, :], pvap1,
                               EhiB[ppar][:, qs], start=pfirst, stop=False)
                        ctx.op("tensor", "matmul", pb[qt].ap[0:64, :], pvap2,
                               EhiB[ppar][:, qs], start=False, stop=False)
                        vt2 = ctx.op("tensor", "matmul", pb[qt].ap[0:65, :],
                                     pvap1, EloB[ppar][:, qs],
                                     start=False, stop=plast, inc="sT")
                        if plast:
                            pb[qt].ready = ("sT", vt2)
                        last_mm = vt2
                    e_free2[ppar] = last_mm
                    if freeslot is not None:
                        kv_freed[freeslot] = ("sT", last_mm)
            q_freed[qslot] = ("sT", last_mm)
            for qt in range(NQT):
                bk = pb[qt]
                par = qt % 2
                ctx.waitp("vector", bk.ready)
                vvr = ctx.op("vector", "reciprocal", rrow[:],
                             bk.ap[64:65, :], inc="sV")
                sbc = pb[6 + par] if NQT == 4 else pb[5]
                ctx.waitp("tensor", sbc.freed)
                ctx.wait("tensor", "sV", vvr)
                vt2 = ctx.op("tensor", "matmul", sbc.ap[0:64, :],
                             ones[0:1, 0:64], rrow[:], start=True, stop=True,
                             inc="sT")
                sbc.ready = ("sT", vt2)
                ctx.wait("vector", "sT", vt2)
                ctx.op("vector", "tensor_copy", bcb[:], sbc.ap[0:64, :])
                ctx.dwait("vector", f"sO2{par}")
                vdv = ctx.op("vector", "tensor_tensor",
                             out=o2st[par][0:64, :],
                             in0=bk.ap[0:64, :], in1=bcb[:], op=OP.mult,
                             inc="sV")
                bk.freed = ("sV", vdv)
                sbc.freed = ("sV", vdv)
                ctx.wait("gpsimd", "sV", vdv)
                for j in range(4):
                    tt = 4 * qt + j
                    ctx.dma("gpsimd", o2_scr[tt, 64 * h:64 * (h + 1), :],
                            o2st[par][0:64, 128 * j:128 * (j + 1)],
                            f"sO2{par}")
        while ws_state["i"] < len(wsplit_jobs):
            wsplit_step()
        ctx.barrier()
        p3s.close()
        wsps.close()

        if stages < 3:
            ctx.finish()
            return finalize(nc, ctx)

        p4s = contextlib.ExitStack()
        E4 = p4s.enter_context
        wWh = E4(nc.sbuf_tensor("wWh4", [128, 8 * D], BF16))
        wWl = E4(nc.sbuf_tensor("wWl4", [128, 8 * D], BF16))
        wraw = E4(nc.sbuf_tensor("wraw4", [128, 8 * D], F32))
        o2r = [E4(nc.sbuf_tensor(f"o2r{i}", [128, D], F32)) for i in range(2)]
        o2h = E4(nc.sbuf_tensor("o2h", [128, D], BF16))
        o2l = E4(nc.sbuf_tensor("o2l", [128, D], BF16))
        xmid = [E4(nc.sbuf_tensor(f"xmid{i}", [128, D], F32)) for i in range(2)]
        tacc = E4(nc.sbuf_tensor("tacc", [128, D], F32))

        def load_weight_ff(wT):
            ctx.dma("sync",
                    wraw[:].rearrange("p (c o) -> p c o", o=D),
                    wT.rearrange("(c p) o -> p c o", p=128), "sIN")
            ctx.dwait("vector", "sIN")
            return split_f32r(wWh, wWl, wraw, 8 * D)

        vw = load_weight_ff(wffT)
        ctx.wait("tensor", "sV", vw)
        ctx.dma("sync", o2r[0][:].rearrange("p (c t) -> p c t", t=128),
                o2_scr[0].rearrange("(c p) t -> p c t", p=128), "sOR0")
        ctx.dma("sync", xb[0][:], xo[0:128, :], "sX0")
        prev_mm = None
        split_done = {}
        xm_used = {}
        xm_stored = {}
        for t in range(NT):
            sl = t % 2
            if t + 1 < NT:
                if (t - 1) in split_done:
                    ctx.wait("sync", "sV", split_done[t - 1])
                if (t - 1) in xm_used:
                    ctx.wait("sync", "sV", xm_used[t - 1])
                ctx.dma(
                    "sync",
                    o2r[(t + 1) % 2][:].rearrange("p (c t) -> p c t", t=128),
                    o2_scr[t + 1].rearrange("(c p) t -> p c t", p=128),
                    f"sOR{(t + 1) % 2}")
                ctx.dma("sync", xb[(t + 1) % 2][:],
                        xo[128 * (t + 1):128 * (t + 2), :],
                        f"sX{(t + 1) % 2}")
            ctx.dwait("vector", f"sOR{sl}")
            if prev_mm is not None:
                ctx.wait("vector", "sT", prev_mm)
            vsp = split_f32r(o2h, o2l, o2r[sl][:], D)
            split_done[t] = vsp
            ctx.wait("tensor", "sV", vsp)
            for blk in range(2):
                bk = pb[blk]
                ctx.waitp("tensor", bk.freed)
                for hc in range(8):
                    vt = mm3(
                        ctx, bk.ap[:, :],
                        o2h[:, 128 * hc:128 * (hc + 1)],
                        o2l[:, 128 * hc:128 * (hc + 1)],
                        wWh[:, D * hc + 512 * blk:D * hc + 512 * blk + 512],
                        wWl[:, D * hc + 512 * blk:D * hc + 512 * blk + 512],
                        start=(hc == 0), stop=(hc == 7),
                        inc="sT" if hc == 7 else None)
                bk.ready = ("sT", vt)
            prev_mm = ctx.tot["sT"]
            ctx.dwait("vector", f"sX{sl}")
            if (t - 2) in xm_stored:
                ctx.dwait("vector", f"sXM{sl}")
            for blk in range(2):
                bk = pb[blk]
                ctx.waitp("vector", bk.ready)
                cs = slice(512 * blk, 512 * (blk + 1))
                ctx.op("vector", "tensor_tensor", out=tacc[:, cs],
                       in0=bk.ap[:, :], in1=bffb[:, cs], op=OP.add)
                vv = ctx.op("vector", "tensor_tensor", out=xmid[sl][:, cs],
                            in0=tacc[:, cs], in1=xb[sl][:, cs], op=OP.add,
                            inc="sV")
                bk.freed = ("sV", vv)
            xm_used[t] = ctx.tot["sV"]
            vb2 = ln_stats(xmid[sl][:], a2[:, t:t + 1], b2[:, t:t + 1])
            ctx.wait("gpsimd", "sV", vb2)
            ctx.dma("gpsimd", xm_scr[128 * t:128 * (t + 1), :],
                    xmid[sl][:], f"sXM{sl}")
            xm_stored[t] = True
        ctx.barrier()
        p4s.close()

        if stages < 4:
            ctx.finish()
            return finalize(nc, ctx)

        p5s = contextlib.ExitStack()
        E5 = p5s.enter_context
        xlnT2h = E5(nc.sbuf_tensor("xlnT2h", [128, 4 * D], BF16))
        xlnT2l = E5(nc.sbuf_tensor("xlnT2l", [128, 4 * D], BF16))
        gTh = E5(nc.sbuf_tensor("gTh", [128, 4 * NI], BF16))
        gTl = E5(nc.sbuf_tensor("gTl", [128, 4 * NI], BF16))
        w1b = [E5(nc.sbuf_tensor(f"w1b{i}", [128, 2 * NI], BF16))
               for i in range(2)]
        w2b = [E5(nc.sbuf_tensor(f"w2b{i}", [128, 8 * D], BF16))
               for i in range(2)]
        ffa = E5(nc.sbuf_tensor("ffa", [128, 4 * D], F32))
        probs = E5(nc.sbuf_tensor("probs", [128, 4 * 9], F32))
        yb = [E5(nc.sbuf_tensor(f"yb{i}", [128, D], F32)) for i in range(2)]

        for sb in range(NSB):
            ctx.barrier()
            xln_used = {}
            norm_done = {}
            for j in range(4):
                t = 4 * sb + j
                sl = j % 2
                if (j - 2) in norm_done:
                    ctx.wait("sync", "sS", norm_done[j - 2])
                ctx.dma("sync", xb[sl][:],
                        xm_scr[128 * t:128 * (t + 1), :], f"sX{sl}")
                ctx.dwait("scalar", f"sX{sl}")
                if (j - 2) in xln_used:
                    ctx.wait("scalar", "sT", xln_used[j - 2])
                vs = ctx.op("scalar", "activation", xln[sl][:], xb[sl][:],
                            AF.Identity, bias=b2[:, t:t + 1],
                            scale=a2[:, t:t + 1], inc="sS")
                norm_done[j] = vs
                ctx.wait("tensor", "sS", vs)
                for g in range(2):
                    bk = pb[6 + g]
                    ctx.waitp("tensor", bk.freed)
                    for q in range(4):
                        c = 4 * g + q
                        vt = ctx.op("tensor", "transpose",
                                    bk.ap[:, 128 * q:128 * (q + 1)],
                                    xln[sl][:, 128 * c:128 * (c + 1)],
                                    ident[:], inc="sT" if q == 3 else None)
                    bk.ready = ("sT", vt)
                xln_used[j] = ctx.tot["sT"]
                for g in range(2):
                    bk = pb[6 + g]
                    ctx.waitp("vector", bk.ready)
                    dst = slice(D * j + 512 * g, D * j + 512 * (g + 1))
                    ctx.op("vector", "tensor_copy", xlnT2h[:, dst],
                           bk.ap[:, :])
                    vfree = ctx.op("vector", "tensor_tensor",
                                   out=xlnT2l[:, dst], in0=bk.ap[:, :],
                                   in1=xlnT2h[:, dst], op=OP.subtract,
                                   inc="sV")
                    bk.freed = ("sV", vfree)
            vxt = ctx.tot["sV"]

            ctx.wait("tensor", "sV", vxt)
            for j in range(4):
                bk = pb[j]
                ctx.waitp("tensor", bk.freed)
                for din in range(8):
                    xs = slice(D * j + 128 * din, D * j + 128 * (din + 1))
                    vt = mm3(ctx, bk.ap[:, 0:NE], xlnT2h[:, xs],
                             xlnT2l[:, xs],
                             wrh[:, NE * din:NE * (din + 1)],
                             wrl[:, NE * din:NE * (din + 1)],
                             start=(din == 0), stop=(din == 7),
                             inc="sT" if din == 7 else None)
                bk.ready = ("sT", vt)
            perms = []
            for j in range(4):
                bk = pb[j]
                ctx.waitp("vector", bk.ready)
                ctx.op("vector", "reduce_max", mv[:, 0:1], bk.ap[:, 0:NE],
                       axis=AX.X)
                vv = ctx.op("vector", "tensor_scalar", out=mv[:, 1:2],
                            in0=mv[:, 0:1], scalar1=-1.0, scalar2=None,
                            op0=OP.mult, inc="sV")
                ctx.wait("scalar", "sV", vv)
                pc = slice(9 * j, 9 * j + 8)
                vs = ctx.op("scalar", "activation", probs[:, pc],
                            bk.ap[:, 0:NE], AF.Exp, bias=mv[:, 1:2], inc="sS")
                ctx.wait("vector", "sS", vs)
                ctx.op("vector", "reduce_sum", mv[:, 2:3], probs[:, pc],
                       axis=AX.X)
                ctx.op("vector", "reciprocal", mv[:, 3:4], mv[:, 2:3])
                vv2 = ctx.op("vector", "tensor_scalar", out=probs[:, pc],
                             in0=probs[:, pc], scalar1=mv[:, 3:4],
                             scalar2=None, op0=OP.mult, inc="sV")
                bk.freed = ("sV", vv2)
                perm = list(range(9 * j, 9 * j + 8))
                spare = 9 * j + 8
                for (i1, i2) in batcher8_pairs():
                    p1, p2 = perm[i1], perm[i2]
                    ctx.op("vector", "tensor_tensor",
                           out=probs[:, spare:spare + 1],
                           in0=probs[:, p1:p1 + 1], in1=probs[:, p2:p2 + 1],
                           op=OP.min)
                    ctx.op("vector", "tensor_tensor",
                           out=probs[:, p1:p1 + 1],
                           in0=probs[:, p1:p1 + 1], in1=probs[:, p2:p2 + 1],
                           op=OP.max)
                    perm[i2] = spare
                    spare = p2
                perms.append(perm)
            vsort = ctx.op("vector", "memset", ffa[:], 0.0, inc="sV")
            ctx.wait("tensor", "sV", vsort)
            ctx.wait("scalar", "sV", vsort)

            w1q = [None, None]
            w2q = [None, None]
            w1_used = [None, None]
            w2_used = [None, None]
            gel_guard = [None, None]

            def load_w1(e, din, slot):
                if w1_used[slot] is not None:
                    ctx.wait("sync", "sT", w1_used[slot])
                r = slice(128 * din, 128 * (din + 1))
                ctx.dma("sync", w1b[slot][:, 0:NI], w1hi_scr[e, r, :],
                        f"sW1{slot}")
                ctx.dma("sync", w1b[slot][:, NI:2 * NI], w1lo_scr[e, r, :],
                        f"sW1{slot}")
                w1q[slot] = True

            def load_w2(e, icg, slot):
                if w2_used[slot] is not None:
                    ctx.wait("sync", "sT", w2_used[slot])
                r = slice(512 * icg, 512 * (icg + 1))
                ctx.dma("sync",
                        w2b[slot][:, 0:4 * D].rearrange("p (c o) -> p c o", o=D),
                        w2hi_scr[e, r, :].rearrange("(c p) o -> p c o",
                                                    p=128), f"sW2{slot}")
                ctx.dma(
                    "sync",
                    w2b[slot][:, 4 * D:8 * D].rearrange("p (c o) -> p c o", o=D),
                    w2lo_scr[e, r, :].rearrange("(c p) o -> p c o", p=128),
                    f"sW2{slot}")
                w2q[slot] = True

            load_w1(0, 0, 0)
            for e in range(NE):
                for ip in range(4):
                    for din in range(8):
                        slot = din % 2
                        if din + 1 < 8:
                            load_w1(e, din + 1, (din + 1) % 2)
                        ctx.dwait("tensor", f"sW1{slot}")
                        for j in range(4):
                            for m in range(2):
                                ib = 2 * ip + m
                                bk = pb[2 * j + m]
                                if din == 0:
                                    ctx.waitp("tensor", bk.freed)
                                xs = slice(D * j + 128 * din,
                                           D * j + 128 * (din + 1))
                                last = (j == 3 and m == 1)
                                vt = mm3(ctx, bk.ap[:, :], xlnT2h[:, xs],
                                         xlnT2l[:, xs],
                                         w1b[slot][:, 512 * ib:512 * (ib + 1)],
                                         w1b[slot][:, NI + 512 * ib:
                                                    NI + 512 * (ib + 1)],
                                         start=(din == 0), stop=(din == 7),
                                         inc="sT" if (last or din == 7)
                                         else None)
                                if din == 7:
                                    bk.ready = ("sT", vt)
                        w1_used[slot] = vt
                    if ip < 3:
                        load_w1(e, 0, 0)
                    for j in range(4):
                        for m in range(2):
                            ib = 2 * ip + m
                            bk = pb[2 * j + m]
                            ctx.waitp("scalar", bk.ready)
                            if gel_guard[m] is not None:
                                ctx.wait("scalar", "sT", gel_guard[m])
                            vs = ctx.op("scalar", "activation",
                                        xln[m][:, 0:512], bk.ap[:, :],
                                        AF.Gelu, inc="sS")
                            ctx.wait("tensor", "sS", vs)
                            for q in range(4):
                                vt2 = ctx.op("tensor", "transpose",
                                             bk.ap[:, 128 * q:128 * (q + 1)],
                                             xln[m][:, 128 * q:128 * (q + 1)],
                                             ident[:],
                                             inc="sT" if q == 3 else None)
                            bk.ready = ("sT", vt2)
                            gel_guard[m] = vt2
                            ctx.waitp("vector", bk.ready)
                            dst = slice(NI * j + 512 * ib,
                                        NI * j + 512 * (ib + 1))
                            ctx.op("vector", "tensor_copy", gTh[:, dst],
                                   bk.ap[:, :])
                            vfree = ctx.op("vector", "tensor_tensor",
                                           out=gTl[:, dst], in0=bk.ap[:, :],
                                           in1=gTh[:, dst], op=OP.subtract,
                                           inc="sV")
                            bk.freed = ("sV", vfree)
                vgel = ctx.tot["sV"]
                ctx.wait("tensor", "sV", vgel)
                load_w2(e, 0, 0)
                for icg in range(8):
                    slot = icg % 2
                    if icg + 1 < 8:
                        load_w2(e, icg + 1, (icg + 1) % 2)
                    ctx.dwait("tensor", f"sW2{slot}")
                    for sub in range(4):
                        ic = 4 * icg + sub
                        for j in range(4):
                            gs = slice(NI * j + 128 * ic,
                                       NI * j + 128 * (ic + 1))
                            for hb in range(2):
                                bk = pb[2 * j + hb]
                                if ic == 0:
                                    ctx.waitp("tensor", bk.freed)
                                wsh = w2b[slot][:, D * sub + 512 * hb:
                                                D * sub + 512 * hb + 512]
                                wslo = w2b[slot][:, 4 * D + D * sub + 512 * hb:
                                                 4 * D + D * sub + 512 * hb + 512]
                                last = (sub == 3 and j == 3 and hb == 1)
                                vt = mm3(ctx, bk.ap[:, :], gTh[:, gs],
                                         gTl[:, gs], wsh, wslo,
                                         start=(ic == 0), stop=(ic == 31),
                                         inc="sT" if (last or ic == 31)
                                         else None)
                                if ic == 31:
                                    bk.ready = ("sT", vt)
                    w2_used[slot] = vt
                for j in range(4):
                    pcol = perms[j][e]
                    for hb in range(2):
                        bk = pb[2 * j + hb]
                        ctx.waitp("vector", bk.ready)
                        dst = slice(D * j + 512 * hb, D * j + 512 * (hb + 1))
                        vv = ctx.op("vector", "scalar_tensor_tensor",
                                    out=ffa[:, dst], in0=bk.ap[:, :],
                                    scalar=probs[:, pcol:pcol + 1],
                                    in1=ffa[:, dst], op0=OP.mult, op1=OP.add,
                                    inc="sV")
                        bk.freed = ("sV", vv)
                if e + 1 < NE:
                    load_w1(e + 1, 0, 0)

            y_done = {}
            yadd_done = {}
            for j in range(4):
                t = 4 * sb + j
                sl = j % 2
                if (j - 2) in yadd_done:
                    ctx.wait("sync", "sV", yadd_done[j - 2])
                else:
                    ctx.wait("sync", "sS", norm_done[2 + sl])
                ctx.dma("sync", xb[sl][:],
                        xm_scr[128 * t:128 * (t + 1), :], f"sX{sl}")
                ctx.dwait("vector", f"sX{sl}")
                if (j - 2) in y_done:
                    ctx.dwait("vector", f"sY{sl}")
                vv = ctx.op("vector", "tensor_tensor", out=yb[sl][:],
                            in0=ffa[:, D * j:D * (j + 1)], in1=xb[sl][:],
                            op=OP.add, inc="sV")
                yadd_done[j] = vv
                ctx.wait("gpsimd", "sV", vv)
                ctx.dma("gpsimd", y[128 * t:128 * (t + 1), :], yb[sl][:],
                        f"sY{sl}")
                y_done[j] = True

        ctx.finish()
        p5s.close()
        return finalize(nc, ctx)


def finalize(nc, ctx):
    with nc.Block() as block:
        for ename in ENGINES:
            def make(ename):
                def prog(eng):
                    for fn in ctx.streams[ename]:
                        fn(eng)
                prog.__name__ = f"prog_{ename}"
                return prog
            getattr(block, ename)(make(ename))
    return nc


_NC_CACHE = {}


def get_program(S=8192, TOK=2048, debug=(), stages=99):
    key = (S, TOK, tuple(sorted(debug)), stages)
    if key not in _NC_CACHE:
        _NC_CACHE[key] = build_program(S, TOK, debug, stages)
    return _NC_CACHE[key]


def make_in_maps(inputs, n_cores=8, S=8192, TOK=2048):
    x = np.ascontiguousarray(np.asarray(inputs["L0_attn_x"], dtype=np.float32))
    B = x.shape[0]
    wqT = np.ascontiguousarray(np.asarray(inputs["L0_attn_Wq"]).T)
    wkT = np.ascontiguousarray(np.asarray(inputs["L0_attn_Wk"]).T)
    wvT = np.ascontiguousarray(np.asarray(inputs["L0_attn_Wv"]).T)
    wffT = np.ascontiguousarray(np.asarray(inputs["L0_attn_Wff"]).T)
    bff = np.ascontiguousarray(np.asarray(inputs["L0_attn_bff"]).reshape(1, D))
    wrT = np.ascontiguousarray(np.asarray(inputs["L0_ffn_Wr"]).T)
    w1T = np.ascontiguousarray(np.asarray(inputs["L0_ffn_W1"]).transpose(0, 2, 1))
    w2T = np.ascontiguousarray(np.asarray(inputs["L0_ffn_W2"]).transpose(0, 2, 1))
    ident = np.eye(128, dtype=np.float32)
    per_batch = n_cores // B
    in_maps = []
    for c in range(n_cores):
        b = c // per_batch
        q = c % per_batch
        xfb = np.ascontiguousarray(x[b])
        xob = np.ascontiguousarray(x[b, TOK * q:TOK * (q + 1)])
        in_maps.append({
            "xf": xfb, "xo": xob, "wqT": wqT, "wkT": wkT, "wvT": wvT,
            "wffT": wffT, "bff": bff, "wrT": wrT, "w1T": w1T, "w2T": w2T,
            "ident": ident,
        })
    return in_maps


def kernel(**inputs):
    S, TOK, NCORES = 8192, 2048, 8
    nc = get_program(S, TOK)
    in_maps = make_in_maps(inputs, NCORES, S, TOK)
    res = run_bass_kernel_spmd(nc, in_maps, core_ids=list(range(NCORES)))
    x = inputs["L0_attn_x"]
    B = x.shape[0]
    out = np.empty((B, S, D), dtype=np.float32)
    per_batch = NCORES // B
    for c in range(NCORES):
        b = c // per_batch
        q = c % per_batch
        out[b, TOK * q:TOK * (q + 1)] = res.results[c]["y"]
    return out


# version 21
# speedup vs baseline: 1.0864x; 1.0442x over previous
import contextlib
import numpy as np

import concourse.bass as bass
import concourse.mybir as mybir
from concourse.bass_utils import run_bass_kernel_spmd

F32 = mybir.dt.float32
F32R = mybir.dt.float32r
BF16 = mybir.dt.bfloat16
AF = mybir.ActivationFunctionType
OP = mybir.AluOpType
AX = mybir.AxisListType

ENGINES = ("sync", "gpsimd", "tensor", "vector", "scalar")

D = 1024
H = 16
DH = 64
NE = 8
NI = 4096
EPS = 1e-5


def batcher8_pairs():
    return [(0, 1), (2, 3), (4, 5), (6, 7),
            (0, 2), (1, 3), (4, 6), (5, 7),
            (1, 2), (5, 6),
            (0, 4), (1, 5), (2, 6), (3, 7),
            (2, 4), (3, 5),
            (1, 2), (3, 4), (5, 6)]


class Ctx:
    def __init__(self, nc, stack):
        self.nc = nc
        self.stack = stack
        self.streams = {e: [] for e in ENGINES}
        self._dirty = {}
        self.sems = {}
        self.tot = {}
        self.dma_sems = set()
        for name in ("sT", "sV", "sS", "sB"):
            self.sems[name] = stack.enter_context(nc.semaphore(name))
            self.tot[name] = 0

    def _get_sem(self, name):
        if name not in self.sems:
            self.sems[name] = self.stack.enter_context(self.nc.semaphore(name))
            self.tot[name] = 0
        return self.sems[name]

    def emit(self, engine, fn):
        self.streams[engine].append(fn)

    def op(self, engine, method, *args, inc=None, incby=1, **kwargs):
        h = self.sems[inc] if inc else None
        if engine in ("vector", "scalar"):
            if self._dirty.get(engine):
                self.emit(engine, lambda eng: eng.drain())
            self._dirty[engine] = method != "nop"

        def fn(eng, method=method, args=args, kwargs=kwargs, h=h, incby=incby):
            inst = getattr(eng, method)(*args, **kwargs)
            if h is not None:
                inst.then_inc(h, incby)

        self.emit(engine, fn)
        if inc:
            self.tot[inc] += incby
            return self.tot[inc]
        return None

    def wait(self, engine, sem, v):
        assert v is not None and v <= self.tot[sem], (engine, sem, v, self.tot[sem])
        if v <= 0:
            return
        h = self.sems[sem]
        self.emit(engine, lambda eng, h=h, v=v: eng.wait_ge(h, v))

    def waitp(self, engine, pair):
        if pair is not None:
            self.wait(engine, pair[0], pair[1])

    def dma(self, qeng, out, in_, sem):
        self._get_sem(sem)
        self.dma_sems.add(sem)
        return self.op(qeng, "dma_start", out=out, in_=in_, inc=sem, incby=16)

    def dwait(self, engine, sem):
        self._get_sem(sem)
        self.dma_sems.add(sem)
        self.wait(engine, sem, self.tot[sem])

    def barrier(self):
        for e in ENGINES:
            if e != "sync":
                self.emit(e, lambda eng: eng.drain())
                self._dirty[e] = False
            self.op(e, "nop", inc="sB")
        vb = self.tot["sB"]
        for e in ENGINES:
            self.wait(e, "sB", vb)
            for s in sorted(self.dma_sems):
                self.wait(e, s, self.tot[s])

    def finish(self):
        for e in ENGINES:
            for s in sorted(self.dma_sems):
                self.wait(e, s, self.tot[s])


class Bank:
    def __init__(self, ap):
        self.ap = ap
        self.ready = None
        self.freed = None


def mm3(ctx, ps_ap, shi, slo, mhi, mlo, start, stop, inc=None):
    ctx.op("tensor", "matmul", ps_ap, shi, mhi, start=start, stop=False)
    ctx.op("tensor", "matmul", ps_ap, shi, mlo, start=False, stop=False)
    return ctx.op("tensor", "matmul", ps_ap, slo, mhi, start=False, stop=stop,
                  inc=inc)


def build_program(S=8192, TOK=2048, debug=(), stages=99):
    nc = bass.Bass("TRN2", target_bir_lowering=False)
    NTF = S // 128
    NT = TOK // 128
    NQT = TOK // 512
    NKC = S // 128
    NKCH = NKC // 2
    NSB = TOK // 512
    assert TOK % 512 == 0 and S % 256 == 0 and NT <= NTF

    def dram_in(name, shape, dtype=F32):
        return nc.dram_tensor(name, shape, dtype, kind="ExternalInput").ap()

    def scratch(name, shape, dtype=F32):
        kind = "ExternalOutput" if name in debug else "Internal"
        return nc.dram_tensor(name, shape, dtype, kind=kind).ap()

    xf = dram_in("xf", [S, D])
    xo = dram_in("xo", [TOK, D])
    wqT = dram_in("wqT", [D, D])
    wkT = dram_in("wkT", [D, D])
    wvT = dram_in("wvT", [D, D])
    wffT = dram_in("wffT", [D, D])
    bffi = dram_in("bff", [1, D])
    wrT = dram_in("wrT", [D, NE])
    w1T = dram_in("w1T", [NE, D, NI])
    w2T = dram_in("w2T", [NE, NI, D])
    ident_in = dram_in("ident", [128, 128])

    y = nc.dram_tensor("y", [TOK, D], F32, kind="ExternalOutput").ap()

    kt_scr = scratch("kt_scr", [D, S])
    v_scr = scratch("v_scr", [S, D])
    qt_scr = scratch("qt_scr", [D, TOK])
    o2_scr = scratch("o2_scr", [NT, D, 128])
    xm_scr = scratch("xm_scr", [TOK, D])
    w1hi_scr = scratch("w1hi_scr", [NE, D, NI], BF16)
    w1lo_scr = scratch("w1lo_scr", [NE, D, NI], BF16)
    w2hi_scr = scratch("w2hi_scr", [NE, NI, D], BF16)
    w2lo_scr = scratch("w2lo_scr", [NE, NI, D], BF16)

    with contextlib.ExitStack() as stack:
        E = stack.enter_context
        ctx = Ctx(nc, stack)

        ident = E(nc.sbuf_tensor("identsb", [128, 128], F32))
        ones = E(nc.sbuf_tensor("ones", [1, 128], F32))
        bffb = E(nc.sbuf_tensor("bffb", [128, D], F32))
        a1 = E(nc.sbuf_tensor("a1", [128, NTF], F32))
        b1 = E(nc.sbuf_tensor("b1", [128, NTF], F32))
        a2 = E(nc.sbuf_tensor("a2", [128, NT], F32))
        b2 = E(nc.sbuf_tensor("b2", [128, NT], F32))
        wrh = E(nc.sbuf_tensor("wrh", [128, 8 * NE], BF16))
        wrl = E(nc.sbuf_tensor("wrl", [128, 8 * NE], BF16))
        xb = [E(nc.sbuf_tensor(f"xb{i}", [128, D], F32)) for i in range(2)]
        stat = E(nc.sbuf_tensor("stat", [128, 16], F32))
        mv = E(nc.sbuf_tensor("mv", [128, 8], F32))

        pb = [Bank(E(nc.psum_tensor(f"pb{i}", [128, 512], F32)))
              for i in range(8)]

        wsplit_jobs = []
        for e in range(NE):
            for r in range(D // 128):
                for cc in range(NI // 2048):
                    rs = slice(128 * r, 128 * (r + 1))
                    cs = slice(2048 * cc, 2048 * (cc + 1))
                    wsplit_jobs.append((w1T[e, rs, cs], w1hi_scr[e, rs, cs],
                                        w1lo_scr[e, rs, cs]))
            for r in range(NI // 256):
                rs = slice(256 * r, 256 * (r + 1))
                wsplit_jobs.append(
                    (w2T[e, rs, :].rearrange("(c p) o -> p c o", p=128),
                     w2hi_scr[e, rs, :].rearrange("(c p) o -> p c o", p=128),
                     w2lo_scr[e, rs, :].rearrange("(c p) o -> p c o", p=128)))
        ws_state = {"i": 0, "busy": [None, None]}

        def wsplit_step():
            i = ws_state["i"]
            if i >= len(wsplit_jobs):
                return
            ws_state["i"] = i + 1
            slot = i % 2
            src, dhi, dlo = wsplit_jobs[i]
            if ws_state["busy"][slot]:
                ctx.dwait("sync", f"sWS{slot}")
            v3 = len(src.shape) == 3
            win = (wps[slot][:].rearrange("p (c o) -> p c o", o=D)
                   if v3 else wps[slot][:])
            ctx.dma("sync", win, src, f"sWP{slot}")
            ctx.dwait("vector", f"sWP{slot}")
            ctx.op("vector", "tensor_copy", wph[slot][:], wps[slot][:])
            vl = ctx.op("vector", "tensor_tensor", out=wpl[slot][:],
                        in0=wps[slot][:], in1=wph[slot][:], op=OP.subtract,
                        inc="sV")
            ctx.wait("gpsimd", "sV", vl)
            hout = (wph[slot][:].rearrange("p (c o) -> p c o", o=D)
                    if v3 else wph[slot][:])
            lout = (wpl[slot][:].rearrange("p (c o) -> p c o", o=D)
                    if v3 else wpl[slot][:])
            ctx.dma("gpsimd", dhi, hout, f"sWS{slot}")
            ctx.dma("gpsimd", dlo, lout, f"sWS{slot}")
            ws_state["busy"][slot] = True

        def ln_stats(xb_ap, a_col, b_col):
            ctx.op("vector", "bn_stats", stat[:, 0:6], xb_ap[:, 0:512])
            ctx.op("vector", "bn_stats", stat[:, 6:12], xb_ap[:, 512:1024])
            ctx.op("vector", "bn_aggr", mv[:, 0:2], stat[:, 0:12])
            ctx.op("vector", "tensor_scalar", out=mv[:, 2:3], in0=mv[:, 1:2],
                   scalar1=EPS, scalar2=None, op0=OP.add)
            vv = ctx.op("vector", "reciprocal", mv[:, 3:4], mv[:, 2:3],
                        inc="sV")
            ctx.wait("scalar", "sV", vv)
            vs = ctx.op("scalar", "activation", a_col, mv[:, 3:4], AF.Sqrt,
                        inc="sS")
            ctx.wait("vector", "sS", vs)
            return ctx.op("vector", "scalar_tensor_tensor", out=b_col,
                          in0=mv[:, 0:1], scalar=-1.0, in1=a_col,
                          op0=OP.mult, op1=OP.mult, inc="sV")

        def transpose8(src_sb, banks, dst_f32, wait_tensor=()):
            for sem, v in wait_tensor:
                ctx.wait("tensor", sem, v)
            for g in range(2):
                bk = banks[g]
                ctx.waitp("tensor", bk.freed)
                for q in range(4):
                    c = g * 4 + q
                    vt = ctx.op("tensor", "transpose",
                                bk.ap[:, 128 * q:128 * (q + 1)],
                                src_sb[:, 128 * c:128 * (c + 1)], ident[:],
                                inc="sT" if q == 3 else None)
                bk.ready = ("sT", vt)
            vv = None
            for g in range(2):
                bk = banks[g]
                ctx.waitp("vector", bk.ready)
                vv = ctx.op("vector", "tensor_copy",
                            dst_f32[:, 512 * g:512 * (g + 1)], bk.ap[:],
                            inc="sV")
                bk.freed = ("sV", vv)
            return vt, vv

        def split_f32r(dst_hi, dst_lo, src, n):
            ctx.op("vector", "tensor_copy", dst_hi[:, 0:n], src[:, 0:n])
            return ctx.op("vector", "tensor_tensor", out=dst_lo[:, 0:n],
                          in0=src[:, 0:n], in1=dst_hi[:, 0:n],
                          op=OP.subtract, inc="sV")

        ctx.dma("sync", ident[:], ident_in, "sIN")
        ctx.dma("sync", xb[0][0:1, :], bffi, "sIN")
        ctx.dma("sync",
                xb[1][:, 0:8 * NE].rearrange("p (c e) -> p c e", e=NE),
                wrT.rearrange("(c p) e -> p c e", p=128), "sIN")
        ctx.op("vector", "memset", ones[:], 1.0)
        ctx.dwait("vector", "sIN")
        ctx.op("vector", "tensor_copy", wrh[:, 0:8 * NE], xb[1][:, 0:8 * NE])
        ctx.op("vector", "tensor_tensor", out=wrl[:, 0:8 * NE],
               in0=xb[1][:, 0:8 * NE], in1=wrh[:, 0:8 * NE], op=OP.subtract)
        vv0 = ctx.op("vector", "nop", inc="sV")
        ctx.dwait("tensor", "sIN")
        ctx.wait("tensor", "sV", vv0)
        vt0 = None
        for g in range(2):
            vt0 = ctx.op("tensor", "matmul", pb[6 + g].ap[:, :], ones[0:1, :],
                         xb[0][0:1, 512 * g:512 * (g + 1)], start=True,
                         stop=True, inc="sT")
        ctx.wait("vector", "sT", vt0)
        for g in range(2):
            ctx.op("vector", "tensor_copy", bffb[:, 512 * g:512 * (g + 1)],
                   pb[6 + g].ap[:])
        ctx.op("vector", "nop", inc="sV")
        ctx.barrier()

        if stages < 1:
            ctx.finish()
            return finalize(nc, ctx)

        xln = [E(nc.sbuf_tensor(f"xln{i}", [128, D], F32)) for i in range(2)]
        p2s = contextlib.ExitStack()
        E2 = p2s.enter_context
        wWh = E2(nc.sbuf_tensor("wWh", [128, 8 * D], BF16))
        wWl = E2(nc.sbuf_tensor("wWl", [128, 8 * D], BF16))
        wraw = E2(nc.sbuf_tensor("wraw", [128, 8 * D], F32))
        wps = [E2(nc.sbuf_tensor(f"wps{i}", [128, 2048], F32)) for i in range(2)]
        wph = [E2(nc.sbuf_tensor(f"wph{i}", [128, 2048], BF16)) for i in range(2)]
        wpl = [E2(nc.sbuf_tensor(f"wpl{i}", [128, 2048], BF16)) for i in range(2)]
        xlT = [E2(nc.sbuf_tensor(f"xlT{i}", [128, D], F32)) for i in range(2)]
        xlh = [E2(nc.sbuf_tensor(f"xlh{i}", [128, D], BF16)) for i in range(2)]
        xll = [E2(nc.sbuf_tensor(f"xll{i}", [128, D], BF16)) for i in range(2)]
        ko = [E2(nc.sbuf_tensor(f"ko{i}", [128, D], F32)) for i in range(2)]
        koT = [E2(nc.sbuf_tensor(f"koT{i}", [128, D], F32)) for i in range(2)]

        def load_weight_T(wT):
            ctx.dma("sync",
                    wraw[:].rearrange("p (c o) -> p c o", o=D),
                    wT.rearrange("(c p) o -> p c o", p=128), "sIN")
            ctx.dwait("vector", "sIN")
            return split_f32r(wWh, wWl, wraw, 8 * D)

        def proj_phase(kind):
            wT = {"K": wkT, "V": wvT, "Q": wqT}[kind]
            ntiles = NTF if kind in ("K", "V") else NT
            xsrc = xf if kind in ("K", "V") else xo
            vw = load_weight_T(wT)
            ctx.wait("tensor", "sV", vw)
            last_ev = {}
            store_done = {}
            tdone = {}

            def stage_a(t):
                sl = t % 2
                if (t - 2) in last_ev:
                    ctx.wait("sync", "sS", last_ev[t - 2])
                ctx.dma("sync", xb[sl][:], xsrc[128 * t:128 * (t + 1), :],
                        f"sX{sl}")
                ctx.dwait("vector", f"sX{sl}")
                if kind in ("K", "Q"):
                    vb = ln_stats(xb[sl][:], a1[:, t:t + 1], b1[:, t:t + 1])
                    ctx.wait("scalar", "sV", vb)
                ctx.dwait("scalar", f"sX{sl}")
                if (t - 2) in tdone:
                    ctx.wait("scalar", "sT", tdone[t - 2])
                last_ev[t] = ctx.op("scalar", "activation", xln[sl][:],
                                    xb[sl][:], AF.Identity,
                                    bias=b1[:, t:t + 1], scale=a1[:, t:t + 1],
                                    inc="sS")

            def stage_kt(t):
                sl = t % 2
                if (t - 2) in store_done:
                    ctx.dwait("vector", f"sST{sl}")
                _, vvk = transpose8(ko[sl][:], [pb[4], pb[5]], koT[sl][:],
                                    wait_tensor=[("sS", evdone[t])])
                ctx.wait("gpsimd", "sV", vvk)
                dst = kt_scr if kind == "K" else qt_scr
                ctx.dma("gpsimd",
                        dst[:, 128 * t:128 * (t + 1)]
                        .rearrange("(c p) t -> p c t", p=128),
                        koT[sl][:].rearrange("p (c t) -> p c t", t=128),
                        f"sST{sl}")
                store_done[t] = True

            evdone = {}
            spl = {}
            stage_a(0)
            tdone[0], _ = transpose8(xln[0][:], [pb[6], pb[7]], xlT[0][:],
                                     wait_tensor=[("sS", last_ev[0])])
            spl[0] = split_f32r(xlh[0], xll[0], xlT[0][:], D)
            for t in range(ntiles):
                sl = t % 2
                if t + 1 < ntiles:
                    nsl = (t + 1) % 2
                    stage_a(t + 1)
                    tdone[t + 1], _ = transpose8(
                        xln[nsl][:], [pb[6], pb[7]], xlT[nsl][:],
                        wait_tensor=[("sS", last_ev[t + 1])])
                    spl[t + 1] = split_f32r(xlh[nsl], xll[nsl], xlT[nsl][:],
                                            D)
                ctx.wait("tensor", "sV", spl[t])
                for blk in range(2):
                    bk = pb[blk]
                    ctx.waitp("tensor", bk.freed)
                    for din in range(8):
                        vt = mm3(
                            ctx, bk.ap[:, :],
                            xlh[sl][:, 128 * din:128 * (din + 1)],
                            xll[sl][:, 128 * din:128 * (din + 1)],
                            wWh[:, D * din + 512 * blk:D * din + 512 * blk + 512],
                            wWl[:, D * din + 512 * blk:D * din + 512 * blk + 512],
                            start=(din == 0), stop=(din == 7),
                            inc="sT" if din == 7 else None)
                    bk.ready = ("sT", vt)
                if kind != "V" and t >= 1:
                    stage_kt(t - 1)
                if (t - 2) in store_done or (kind == "V" and (t - 2) in evdone):
                    ctx.dwait("scalar", f"sST{sl}")
                for blk in range(2):
                    bk = pb[blk]
                    ctx.waitp("scalar", bk.ready)
                    sc = (1.0 / 32.0) if kind == "Q" else 1.0
                    vs = ctx.op("scalar", "activation",
                                ko[sl][:, 512 * blk:512 * (blk + 1)],
                                bk.ap[:], AF.Copy, scale=sc, inc="sS")
                    bk.freed = ("sS", vs)
                evdone[t] = ctx.tot["sS"]
                if kind == "V":
                    ctx.wait("gpsimd", "sS", evdone[t])
                    ctx.dma("gpsimd", v_scr[128 * t:128 * (t + 1), :],
                            ko[sl][:], f"sST{sl}")
                    store_done[t] = True
                wsplit_step()
            if kind != "V":
                stage_kt(ntiles - 1)

        proj_phase("K")
        ctx.barrier()
        proj_phase("V")
        ctx.barrier()
        proj_phase("Q")
        while ws_state["i"] < len(wsplit_jobs):
            wsplit_step()
        ctx.barrier()
        p2s.close()

        if stages < 2:
            ctx.finish()
            return finalize(nc, ctx)

        SH = S // 2
        p3s = contextlib.ExitStack()
        E3 = p3s.enter_context
        kraw = [E3(nc.sbuf_tensor(f"kraw{i}", [128, SH], F32))
                for i in range(2)]
        kstk = [E3(nc.sbuf_tensor(f"kstk{i}", [128, SH], BF16))
                for i in range(2)]
        vraw = [E3(nc.sbuf_tensor(f"vraw{i}", [128, NKCH * 64], F32))
                for i in range(2)]
        vst1 = [E3(nc.sbuf_tensor(f"vst1{i}", [128, NKCH * 65], BF16))
                for i in range(2)]
        vst2 = [E3(nc.sbuf_tensor(f"vst2{i}", [128, NKCH * 64], BF16))
                for i in range(2)]
        qst = [E3(nc.sbuf_tensor(f"qst{i}", [128, TOK], F32)) for i in range(2)]
        qhi = [E3(nc.sbuf_tensor(f"qhi{i}", [128, TOK], BF16))
               for i in range(2)]
        qlo = [E3(nc.sbuf_tensor(f"qlo{i}", [128, TOK], BF16))
               for i in range(2)]
        Eb = [E3(nc.sbuf_tensor(f"Eb{i}", [128, 2048], F32))
              for i in range(3)]
        EhiB = [E3(nc.sbuf_tensor(f"Ehi{i}", [128, 2048], BF16))
                for i in range(3)]
        EloB = [E3(nc.sbuf_tensor(f"Elo{i}", [128, 2048], BF16))
                for i in range(3)]
        rrow = E3(nc.sbuf_tensor("rrow", [1, 512], F32))
        bcb = E3(nc.sbuf_tensor("bcb", [64, 512], F32))
        o2st = [E3(nc.sbuf_tensor(f"o2st{i}", [128, 512], F32))
                for i in range(2)]

        for i in range(2):
            ctx.op("vector", "memset",
                   vst1[i][:].rearrange("p (k c) -> p k c", c=65)
                   [:, :, 64:65], 1.0)
        vv_ones = ctx.op("vector", "nop", inc="sV")
        ctx.wait("gpsimd", "sV", vv_ones)
        ctx.wait("tensor", "sV", vv_ones)
        ctx._get_sem("sP")

        def gop(method, *a, **k):
            ctx.wait("gpsimd", "sP", ctx.tot["sP"])
            return ctx.op("gpsimd", method, *a, inc="sP", **k)

        kv_freed = [None, None]
        q_freed = [None, None]
        kv_ready = [None, None]
        q_ready = [None, None]

        def load_head_kv(h, half, slot):
            if kv_freed[slot] is not None:
                ctx.wait("sync", *kv_freed[slot])
            src = kt_scr[64 * h:64 * (h + 1), SH * half:SH * (half + 1)]
            ctx.dma("sync", kraw[slot][0:64, :], src, f"sKV{slot}")
            ctx.dma("sync", kraw[slot][64:128, :], src, f"sKV{slot}")
            G = 4
            for g in range(NKCH // G):
                rows = v_scr[SH * half + 128 * G * g:
                             SH * half + 128 * G * (g + 1),
                             64 * h:64 * (h + 1)]
                ctx.dma("sync",
                        vraw[slot][:].rearrange("p (k c) -> p k c", c=64)
                        [:, G * g:G * (g + 1), :],
                        rows.rearrange("(k p) c -> p k c", p=128),
                        f"sKV{slot}")
            ctx.dwait("gpsimd", f"sKV{slot}")
            gop("tensor_copy", kstk[slot][0:64, :], kraw[slot][0:64, :])
            gop("tensor_copy", kstk[slot][64:128, :], kraw[slot][64:128, :])
            gop("tensor_tensor", out=kstk[slot][64:128, :],
                in0=kraw[slot][64:128, :], in1=kstk[slot][64:128, :],
                op=OP.subtract)
            vv3 = vraw[slot][:].rearrange("p (k c) -> p k c", c=64)
            vh3 = vst1[slot][:].rearrange("p (k c) -> p k c", c=65)[:, :, 0:64]
            vl3 = vst2[slot][:].rearrange("p (k c) -> p k c", c=64)
            gop("tensor_copy", vh3, vv3)
            vv = gop("tensor_tensor", out=vl3, in0=vv3, in1=vh3,
                     op=OP.subtract)
            kv_ready[slot] = ("sP", vv)

        def load_head_q(h, slot):
            if q_freed[slot] is not None:
                ctx.wait("sync", *q_freed[slot])
            src = qt_scr[64 * h:64 * (h + 1), :]
            ctx.dma("sync", qst[slot][0:64, :], src, f"sQL{slot}")
            ctx.dma("sync", qst[slot][64:128, :], src, f"sQL{slot}")
            ctx.dwait("vector", f"sQL{slot}")
            ctx.op("vector", "tensor_copy", qhi[slot][:], qst[slot][:])
            vv = ctx.op("vector", "tensor_tensor", out=qlo[slot][:],
                        in0=qst[slot][:], in1=qhi[slot][:], op=OP.subtract,
                        inc="sV")
            q_ready[slot] = ("sV", vv)

        load_head_q(0, 0)
        load_head_kv(0, 0, 0)
        e_free2 = [None, None, None]
        eb_users = [[], [], []]
        ehilo_v = [None, None, None]

        for h in range(H):
            qslot = h % 2
            if h + 1 < H:
                load_head_q(h + 1, (h + 1) % 2)
            ctx.waitp("tensor", q_ready[qslot])
            for qt in range(NQT):
                ctx.waitp("tensor", pb[qt].freed)
            last_mm = None
            pipe = []
            for kci in range(NKC + 2):
                if kci < NKC:
                    half = kci // NKCH
                    kc = kci % NKCH
                    slot = (2 * h + half) % 2
                    par = kci % 3
                    if kc == 0:
                        ctx.waitp("tensor", kv_ready[slot])
                    if kc == 2:
                        if half == 0:
                            load_head_kv(h, 1, slot ^ 1)
                        elif h + 1 < H:
                            load_head_kv(h + 1, 0, slot ^ 1)
                    kap = kstk[slot][:, 128 * kc:128 * (kc + 1)]
                    for qt in range(NQT):
                        sbk = pb[4 + qt]
                        ctx.waitp("tensor", sbk.freed)
                        ctx.op("tensor", "matmul", sbk.ap[:, :], kap,
                               qhi[qslot][:, 512 * qt:512 * (qt + 1)],
                               start=True, stop=False)
                        vt = ctx.op("tensor", "matmul", sbk.ap[:, :], kap,
                                    qlo[qslot][:, 512 * qt:512 * (qt + 1)],
                                    start=False, stop=True, inc="sT")
                        sbk.ready = ("sT", vt)
                    for pr in eb_users[par]:
                        ctx.waitp("scalar", pr)
                    for qt in range(NQT):
                        sbk = pb[4 + qt]
                        ctx.waitp("scalar", sbk.ready)
                        vexp = ctx.op("scalar", "activation",
                                      Eb[par][:, 512 * qt:512 * (qt + 1)],
                                      sbk.ap[:], AF.Exp, inc="sS")
                        sbk.freed = ("sS", vexp)
                    if e_free2[par] is not None:
                        ctx.wait("scalar", "sT", e_free2[par])
                    vhi = ctx.op("scalar", "activation",
                                 EhiB[par][:, 0:512 * NQT],
                                 Eb[par][:, 0:512 * NQT], AF.Copy, inc="sS")
                    ctx.wait("vector", "sS", vhi)
                    if e_free2[par] is not None:
                        ctx.wait("vector", "sT", e_free2[par])
                    vlo = ctx.op("vector", "scalar_tensor_tensor",
                                 out=EloB[par][:, 0:512 * NQT],
                                 in0=Eb[par][:, 0:512 * NQT],
                                 scalar=1.0, in1=EhiB[par][:, 0:512 * NQT],
                                 op0=OP.mult, op1=OP.subtract, inc="sV")
                    ehilo_v[par] = (vhi, vlo)
                    eb_users[par] = [("sS", vhi), ("sV", vlo)]
                    pipe.append((vst1[slot][:, 65 * kc:65 * kc + 65],
                                 vst2[slot][:, 64 * kc:64 * kc + 64],
                                 par, kci == 0, kci == NKC - 1,
                                 slot if kc == NKCH - 1 else None))
                if kci >= 2:
                    pvap1, pvap2, ppar, pfirst, plast, freeslot = pipe.pop(0)
                    ctx.wait("tensor", "sS", ehilo_v[ppar][0])
                    ctx.wait("tensor", "sV", ehilo_v[ppar][1])
                    for qt in range(NQT):
                        qs = slice(512 * qt, 512 * (qt + 1))
                        ctx.op("tensor", "matmul", pb[qt].ap[0:65, :], pvap1,
                               EhiB[ppar][:, qs], start=pfirst, stop=False)
                        ctx.op("tensor", "matmul", pb[qt].ap[0:64, :], pvap2,
                               EhiB[ppar][:, qs], start=False, stop=False)
                        vt2 = ctx.op("tensor", "matmul", pb[qt].ap[0:65, :],
                                     pvap1, EloB[ppar][:, qs],
                                     start=False, stop=plast, inc="sT")
                        if plast:
                            pb[qt].ready = ("sT", vt2)
                        last_mm = vt2
                    e_free2[ppar] = last_mm
                    if freeslot is not None:
                        kv_freed[freeslot] = ("sT", last_mm)
            q_freed[qslot] = ("sT", last_mm)
            for qt in range(NQT):
                bk = pb[qt]
                par = qt % 2
                ctx.waitp("vector", bk.ready)
                vvr = ctx.op("vector", "reciprocal", rrow[:],
                             bk.ap[64:65, :], inc="sV")
                sbc = pb[6 + par] if NQT == 4 else pb[5]
                ctx.waitp("tensor", sbc.freed)
                ctx.wait("tensor", "sV", vvr)
                vt2 = ctx.op("tensor", "matmul", sbc.ap[0:64, :],
                             ones[0:1, 0:64], rrow[:], start=True, stop=True,
                             inc="sT")
                sbc.ready = ("sT", vt2)
                ctx.wait("vector", "sT", vt2)
                ctx.op("vector", "tensor_copy", bcb[:], sbc.ap[0:64, :])
                ctx.dwait("vector", f"sO2{par}")
                vdv = ctx.op("vector", "tensor_tensor",
                             out=o2st[par][0:64, :],
                             in0=bk.ap[0:64, :], in1=bcb[:], op=OP.mult,
                             inc="sV")
                bk.freed = ("sV", vdv)
                sbc.freed = ("sV", vdv)
                ctx.wait("gpsimd", "sV", vdv)
                ctx.dma("gpsimd",
                        o2_scr[4 * qt:4 * qt + 4, 64 * h:64 * (h + 1), :]
                        .rearrange("a b c -> b a c"),
                        o2st[par][0:64, :].rearrange("p (j t) -> p j t",
                                                     t=128),
                        f"sO2{par}")
        ctx.barrier()
        p3s.close()

        if stages < 3:
            ctx.finish()
            return finalize(nc, ctx)

        p4s = contextlib.ExitStack()
        E4 = p4s.enter_context
        wWh = E4(nc.sbuf_tensor("wWh4", [128, 8 * D], BF16))
        wWl = E4(nc.sbuf_tensor("wWl4", [128, 8 * D], BF16))
        wraw = E4(nc.sbuf_tensor("wraw4", [128, 8 * D], F32))
        o2r = [E4(nc.sbuf_tensor(f"o2r{i}", [128, D], F32)) for i in range(2)]
        o2h = E4(nc.sbuf_tensor("o2h", [128, D], BF16))
        o2l = E4(nc.sbuf_tensor("o2l", [128, D], BF16))
        xmid = [E4(nc.sbuf_tensor(f"xmid{i}", [128, D], F32)) for i in range(2)]
        tacc = E4(nc.sbuf_tensor("tacc", [128, D], F32))

        def load_weight_ff(wT):
            ctx.dma("sync",
                    wraw[:].rearrange("p (c o) -> p c o", o=D),
                    wT.rearrange("(c p) o -> p c o", p=128), "sIN")
            ctx.dwait("vector", "sIN")
            return split_f32r(wWh, wWl, wraw, 8 * D)

        vw = load_weight_ff(wffT)
        ctx.wait("tensor", "sV", vw)
        ctx.dma("sync", o2r[0][:].rearrange("p (c t) -> p c t", t=128),
                o2_scr[0].rearrange("(c p) t -> p c t", p=128), "sOR0")
        ctx.dma("sync", xb[0][:], xo[0:128, :], "sX0")
        prev_mm = None
        split_done = {}
        xm_used = {}
        xm_stored = {}
        for t in range(NT):
            sl = t % 2
            if t + 1 < NT:
                if (t - 1) in split_done:
                    ctx.wait("sync", "sV", split_done[t - 1])
                if (t - 1) in xm_used:
                    ctx.wait("sync", "sV", xm_used[t - 1])
                ctx.dma(
                    "sync",
                    o2r[(t + 1) % 2][:].rearrange("p (c t) -> p c t", t=128),
                    o2_scr[t + 1].rearrange("(c p) t -> p c t", p=128),
                    f"sOR{(t + 1) % 2}")
                ctx.dma("sync", xb[(t + 1) % 2][:],
                        xo[128 * (t + 1):128 * (t + 2), :],
                        f"sX{(t + 1) % 2}")
            ctx.dwait("vector", f"sOR{sl}")
            if prev_mm is not None:
                ctx.wait("vector", "sT", prev_mm)
            vsp = split_f32r(o2h, o2l, o2r[sl][:], D)
            split_done[t] = vsp
            ctx.wait("tensor", "sV", vsp)
            for blk in range(2):
                bk = pb[blk]
                ctx.waitp("tensor", bk.freed)
                for hc in range(8):
                    vt = mm3(
                        ctx, bk.ap[:, :],
                        o2h[:, 128 * hc:128 * (hc + 1)],
                        o2l[:, 128 * hc:128 * (hc + 1)],
                        wWh[:, D * hc + 512 * blk:D * hc + 512 * blk + 512],
                        wWl[:, D * hc + 512 * blk:D * hc + 512 * blk + 512],
                        start=(hc == 0), stop=(hc == 7),
                        inc="sT" if hc == 7 else None)
                bk.ready = ("sT", vt)
            prev_mm = ctx.tot["sT"]
            ctx.dwait("vector", f"sX{sl}")
            if (t - 2) in xm_stored:
                ctx.dwait("vector", f"sXM{sl}")
            for blk in range(2):
                bk = pb[blk]
                ctx.waitp("vector", bk.ready)
                cs = slice(512 * blk, 512 * (blk + 1))
                ctx.op("vector", "tensor_tensor", out=tacc[:, cs],
                       in0=bk.ap[:, :], in1=bffb[:, cs], op=OP.add)
                vv = ctx.op("vector", "tensor_tensor", out=xmid[sl][:, cs],
                            in0=tacc[:, cs], in1=xb[sl][:, cs], op=OP.add,
                            inc="sV")
                bk.freed = ("sV", vv)
            xm_used[t] = ctx.tot["sV"]
            vb2 = ln_stats(xmid[sl][:], a2[:, t:t + 1], b2[:, t:t + 1])
            ctx.wait("gpsimd", "sV", vb2)
            ctx.dma("gpsimd", xm_scr[128 * t:128 * (t + 1), :],
                    xmid[sl][:], f"sXM{sl}")
            xm_stored[t] = True
        ctx.barrier()
        p4s.close()

        if stages < 4:
            ctx.finish()
            return finalize(nc, ctx)

        p5s = contextlib.ExitStack()
        E5 = p5s.enter_context
        xlnT2h = E5(nc.sbuf_tensor("xlnT2h", [128, 4 * D], BF16))
        xlnT2l = E5(nc.sbuf_tensor("xlnT2l", [128, 4 * D], BF16))
        gTh = E5(nc.sbuf_tensor("gTh", [128, 4 * NI], BF16))
        gTl = E5(nc.sbuf_tensor("gTl", [128, 4 * NI], BF16))
        w1b = [E5(nc.sbuf_tensor(f"w1b{i}", [128, 2 * NI], BF16))
               for i in range(2)]
        w2b = [E5(nc.sbuf_tensor(f"w2b{i}", [128, 8 * D], BF16))
               for i in range(2)]
        ffa = E5(nc.sbuf_tensor("ffa", [128, 4 * D], F32))
        probs = E5(nc.sbuf_tensor("probs", [128, 4 * 9], F32))
        yb = [E5(nc.sbuf_tensor(f"yb{i}", [128, D], F32)) for i in range(2)]

        for sb in range(NSB):
            ctx.barrier()
            xln_used = {}
            norm_done = {}
            for j in range(4):
                t = 4 * sb + j
                sl = j % 2
                if (j - 2) in norm_done:
                    ctx.wait("sync", "sS", norm_done[j - 2])
                ctx.dma("sync", xb[sl][:],
                        xm_scr[128 * t:128 * (t + 1), :], f"sX{sl}")
                ctx.dwait("scalar", f"sX{sl}")
                if (j - 2) in xln_used:
                    ctx.wait("scalar", "sT", xln_used[j - 2])
                vs = ctx.op("scalar", "activation", xln[sl][:], xb[sl][:],
                            AF.Identity, bias=b2[:, t:t + 1],
                            scale=a2[:, t:t + 1], inc="sS")
                norm_done[j] = vs
                ctx.wait("tensor", "sS", vs)
                for g in range(2):
                    bk = pb[6 + g]
                    ctx.waitp("tensor", bk.freed)
                    for q in range(4):
                        c = 4 * g + q
                        vt = ctx.op("tensor", "transpose",
                                    bk.ap[:, 128 * q:128 * (q + 1)],
                                    xln[sl][:, 128 * c:128 * (c + 1)],
                                    ident[:], inc="sT" if q == 3 else None)
                    bk.ready = ("sT", vt)
                xln_used[j] = ctx.tot["sT"]
                for g in range(2):
                    bk = pb[6 + g]
                    ctx.waitp("vector", bk.ready)
                    dst = slice(D * j + 512 * g, D * j + 512 * (g + 1))
                    ctx.op("vector", "tensor_copy", xlnT2h[:, dst],
                           bk.ap[:, :])
                    vfree = ctx.op("vector", "tensor_tensor",
                                   out=xlnT2l[:, dst], in0=bk.ap[:, :],
                                   in1=xlnT2h[:, dst], op=OP.subtract,
                                   inc="sV")
                    bk.freed = ("sV", vfree)
            vxt = ctx.tot["sV"]

            ctx.wait("tensor", "sV", vxt)
            for j in range(4):
                bk = pb[j]
                ctx.waitp("tensor", bk.freed)
                for din in range(8):
                    xs = slice(D * j + 128 * din, D * j + 128 * (din + 1))
                    vt = mm3(ctx, bk.ap[:, 0:NE], xlnT2h[:, xs],
                             xlnT2l[:, xs],
                             wrh[:, NE * din:NE * (din + 1)],
                             wrl[:, NE * din:NE * (din + 1)],
                             start=(din == 0), stop=(din == 7),
                             inc="sT" if din == 7 else None)
                bk.ready = ("sT", vt)
            perms = []
            for j in range(4):
                bk = pb[j]
                ctx.waitp("vector", bk.ready)
                ctx.op("vector", "reduce_max", mv[:, 0:1], bk.ap[:, 0:NE],
                       axis=AX.X)
                vv = ctx.op("vector", "tensor_scalar", out=mv[:, 1:2],
                            in0=mv[:, 0:1], scalar1=-1.0, scalar2=None,
                            op0=OP.mult, inc="sV")
                ctx.wait("scalar", "sV", vv)
                pc = slice(9 * j, 9 * j + 8)
                vs = ctx.op("scalar", "activation", probs[:, pc],
                            bk.ap[:, 0:NE], AF.Exp, bias=mv[:, 1:2], inc="sS")
                ctx.wait("vector", "sS", vs)
                ctx.op("vector", "reduce_sum", mv[:, 2:3], probs[:, pc],
                       axis=AX.X)
                ctx.op("vector", "reciprocal", mv[:, 3:4], mv[:, 2:3])
                vv2 = ctx.op("vector", "tensor_scalar", out=probs[:, pc],
                             in0=probs[:, pc], scalar1=mv[:, 3:4],
                             scalar2=None, op0=OP.mult, inc="sV")
                bk.freed = ("sV", vv2)
                perm = list(range(9 * j, 9 * j + 8))
                spare = 9 * j + 8
                for (i1, i2) in batcher8_pairs():
                    p1, p2 = perm[i1], perm[i2]
                    ctx.op("vector", "tensor_tensor",
                           out=probs[:, spare:spare + 1],
                           in0=probs[:, p1:p1 + 1], in1=probs[:, p2:p2 + 1],
                           op=OP.min)
                    ctx.op("vector", "tensor_tensor",
                           out=probs[:, p1:p1 + 1],
                           in0=probs[:, p1:p1 + 1], in1=probs[:, p2:p2 + 1],
                           op=OP.max)
                    perm[i2] = spare
                    spare = p2
                perms.append(perm)
            vsort = ctx.op("vector", "memset", ffa[:], 0.0, inc="sV")
            ctx.wait("tensor", "sV", vsort)
            ctx.wait("scalar", "sV", vsort)

            w1q = [None, None]
            w2q = [None, None]
            w1_used = [None, None]
            w2_used = [None, None]
            gel_guard = [None, None]

            def load_w1(e, din, slot):
                if w1_used[slot] is not None:
                    ctx.wait("sync", "sT", w1_used[slot])
                r = slice(128 * din, 128 * (din + 1))
                ctx.dma("sync", w1b[slot][:, 0:NI], w1hi_scr[e, r, :],
                        f"sW1{slot}")
                ctx.dma("sync", w1b[slot][:, NI:2 * NI], w1lo_scr[e, r, :],
                        f"sW1{slot}")
                w1q[slot] = True

            def load_w2(e, icg, slot):
                if w2_used[slot] is not None:
                    ctx.wait("sync", "sT", w2_used[slot])
                r = slice(512 * icg, 512 * (icg + 1))
                ctx.dma("sync",
                        w2b[slot][:, 0:4 * D].rearrange("p (c o) -> p c o", o=D),
                        w2hi_scr[e, r, :].rearrange("(c p) o -> p c o",
                                                    p=128), f"sW2{slot}")
                ctx.dma(
                    "sync",
                    w2b[slot][:, 4 * D:8 * D].rearrange("p (c o) -> p c o", o=D),
                    w2lo_scr[e, r, :].rearrange("(c p) o -> p c o", p=128),
                    f"sW2{slot}")
                w2q[slot] = True

            load_w1(0, 0, 0)
            for e in range(NE):
                for ip in range(4):
                    for din in range(8):
                        slot = din % 2
                        if din + 1 < 8:
                            load_w1(e, din + 1, (din + 1) % 2)
                        ctx.dwait("tensor", f"sW1{slot}")
                        for j in range(4):
                            for m in range(2):
                                ib = 2 * ip + m
                                bk = pb[2 * j + m]
                                if din == 0:
                                    ctx.waitp("tensor", bk.freed)
                                xs = slice(D * j + 128 * din,
                                           D * j + 128 * (din + 1))
                                last = (j == 3 and m == 1)
                                vt = mm3(ctx, bk.ap[:, :], xlnT2h[:, xs],
                                         xlnT2l[:, xs],
                                         w1b[slot][:, 512 * ib:512 * (ib + 1)],
                                         w1b[slot][:, NI + 512 * ib:
                                                    NI + 512 * (ib + 1)],
                                         start=(din == 0), stop=(din == 7),
                                         inc="sT" if (last or din == 7)
                                         else None)
                                if din == 7:
                                    bk.ready = ("sT", vt)
                        w1_used[slot] = vt
                    if ip < 3:
                        load_w1(e, 0, 0)
                    for j in range(4):
                        for m in range(2):
                            ib = 2 * ip + m
                            bk = pb[2 * j + m]
                            ctx.waitp("scalar", bk.ready)
                            if gel_guard[m] is not None:
                                ctx.wait("scalar", "sT", gel_guard[m])
                            vs = ctx.op("scalar", "activation",
                                        xln[m][:, 0:512], bk.ap[:, :],
                                        AF.Gelu, inc="sS")
                            ctx.wait("tensor", "sS", vs)
                            for q in range(4):
                                vt2 = ctx.op("tensor", "transpose",
                                             bk.ap[:, 128 * q:128 * (q + 1)],
                                             xln[m][:, 128 * q:128 * (q + 1)],
                                             ident[:],
                                             inc="sT" if q == 3 else None)
                            bk.ready = ("sT", vt2)
                            gel_guard[m] = vt2
                            ctx.waitp("vector", bk.ready)
                            dst = slice(NI * j + 512 * ib,
                                        NI * j + 512 * (ib + 1))
                            ctx.op("vector", "tensor_copy", gTh[:, dst],
                                   bk.ap[:, :])
                            vfree = ctx.op("vector", "tensor_tensor",
                                           out=gTl[:, dst], in0=bk.ap[:, :],
                                           in1=gTh[:, dst], op=OP.subtract,
                                           inc="sV")
                            bk.freed = ("sV", vfree)
                vgel = ctx.tot["sV"]
                ctx.wait("tensor", "sV", vgel)
                load_w2(e, 0, 0)
                for icg in range(8):
                    slot = icg % 2
                    if icg + 1 < 8:
                        load_w2(e, icg + 1, (icg + 1) % 2)
                    ctx.dwait("tensor", f"sW2{slot}")
                    for sub in range(4):
                        ic = 4 * icg + sub
                        for j in range(4):
                            gs = slice(NI * j + 128 * ic,
                                       NI * j + 128 * (ic + 1))
                            for hb in range(2):
                                bk = pb[2 * j + hb]
                                if ic == 0:
                                    ctx.waitp("tensor", bk.freed)
                                wsh = w2b[slot][:, D * sub + 512 * hb:
                                                D * sub + 512 * hb + 512]
                                wslo = w2b[slot][:, 4 * D + D * sub + 512 * hb:
                                                 4 * D + D * sub + 512 * hb + 512]
                                last = (sub == 3 and j == 3 and hb == 1)
                                vt = mm3(ctx, bk.ap[:, :], gTh[:, gs],
                                         gTl[:, gs], wsh, wslo,
                                         start=(ic == 0), stop=(ic == 31),
                                         inc="sT" if (last or ic == 31)
                                         else None)
                                if ic == 31:
                                    bk.ready = ("sT", vt)
                    w2_used[slot] = vt
                for j in range(4):
                    pcol = perms[j][e]
                    for hb in range(2):
                        bk = pb[2 * j + hb]
                        ctx.waitp("vector", bk.ready)
                        dst = slice(D * j + 512 * hb, D * j + 512 * (hb + 1))
                        vv = ctx.op("vector", "scalar_tensor_tensor",
                                    out=ffa[:, dst], in0=bk.ap[:, :],
                                    scalar=probs[:, pcol:pcol + 1],
                                    in1=ffa[:, dst], op0=OP.mult, op1=OP.add,
                                    inc="sV")
                        bk.freed = ("sV", vv)
                if e + 1 < NE:
                    load_w1(e + 1, 0, 0)

            y_done = {}
            yadd_done = {}
            for j in range(4):
                t = 4 * sb + j
                sl = j % 2
                if (j - 2) in yadd_done:
                    ctx.wait("sync", "sV", yadd_done[j - 2])
                else:
                    ctx.wait("sync", "sS", norm_done[2 + sl])
                ctx.dma("sync", xb[sl][:],
                        xm_scr[128 * t:128 * (t + 1), :], f"sX{sl}")
                ctx.dwait("vector", f"sX{sl}")
                if (j - 2) in y_done:
                    ctx.dwait("vector", f"sY{sl}")
                vv = ctx.op("vector", "tensor_tensor", out=yb[sl][:],
                            in0=ffa[:, D * j:D * (j + 1)], in1=xb[sl][:],
                            op=OP.add, inc="sV")
                yadd_done[j] = vv
                ctx.wait("gpsimd", "sV", vv)
                ctx.dma("gpsimd", y[128 * t:128 * (t + 1), :], yb[sl][:],
                        f"sY{sl}")
                y_done[j] = True

        ctx.finish()
        p5s.close()
        return finalize(nc, ctx)


def finalize(nc, ctx):
    with nc.Block() as block:
        for ename in ENGINES:
            def make(ename):
                def prog(eng):
                    for fn in ctx.streams[ename]:
                        fn(eng)
                prog.__name__ = f"prog_{ename}"
                return prog
            getattr(block, ename)(make(ename))
    return nc


_NC_CACHE = {}


def get_program(S=8192, TOK=2048, debug=(), stages=99):
    key = (S, TOK, tuple(sorted(debug)), stages)
    if key not in _NC_CACHE:
        _NC_CACHE[key] = build_program(S, TOK, debug, stages)
    return _NC_CACHE[key]


def make_in_maps(inputs, n_cores=8, S=8192, TOK=2048):
    x = np.ascontiguousarray(np.asarray(inputs["L0_attn_x"], dtype=np.float32))
    B = x.shape[0]
    wqT = np.ascontiguousarray(np.asarray(inputs["L0_attn_Wq"]).T)
    wkT = np.ascontiguousarray(np.asarray(inputs["L0_attn_Wk"]).T)
    wvT = np.ascontiguousarray(np.asarray(inputs["L0_attn_Wv"]).T)
    wffT = np.ascontiguousarray(np.asarray(inputs["L0_attn_Wff"]).T)
    bff = np.ascontiguousarray(np.asarray(inputs["L0_attn_bff"]).reshape(1, D))
    wrT = np.ascontiguousarray(np.asarray(inputs["L0_ffn_Wr"]).T)
    w1T = np.ascontiguousarray(np.asarray(inputs["L0_ffn_W1"]).transpose(0, 2, 1))
    w2T = np.ascontiguousarray(np.asarray(inputs["L0_ffn_W2"]).transpose(0, 2, 1))
    ident = np.eye(128, dtype=np.float32)
    per_batch = n_cores // B
    in_maps = []
    for c in range(n_cores):
        b = c // per_batch
        q = c % per_batch
        xfb = np.ascontiguousarray(x[b])
        xob = np.ascontiguousarray(x[b, TOK * q:TOK * (q + 1)])
        in_maps.append({
            "xf": xfb, "xo": xob, "wqT": wqT, "wkT": wkT, "wvT": wvT,
            "wffT": wffT, "bff": bff, "wrT": wrT, "w1T": w1T, "w2T": w2T,
            "ident": ident,
        })
    return in_maps


def kernel(**inputs):
    S, TOK, NCORES = 8192, 2048, 8
    nc = get_program(S, TOK)
    in_maps = make_in_maps(inputs, NCORES, S, TOK)
    res = run_bass_kernel_spmd(nc, in_maps, core_ids=list(range(NCORES)))
    x = inputs["L0_attn_x"]
    B = x.shape[0]
    out = np.empty((B, S, D), dtype=np.float32)
    per_batch = NCORES // B
    for c in range(NCORES):
        b = c // per_batch
        q = c % per_batch
        out[b, TOK * q:TOK * (q + 1)] = res.results[c]["y"]
    return out
